# version 02
# baseline (speedup 1.0000x reference)
import math
import numpy as np
import ml_dtypes
import concourse.bass as bass
import concourse.mybir as mybir
from concourse.bass_utils import run_bass_kernel_spmd

F32 = mybir.dt.float32
BF16 = mybir.dt.bfloat16
I32 = mybir.dt.int32
U8 = mybir.dt.uint8
AF = mybir.ActivationFunctionType
ALU = mybir.AluOpType
bfdt = ml_dtypes.bfloat16

D = 4096
B = 2
T = 4096
TOK = B * T
NCORES = 8
TPC = TOK // NCORES
DC = D // 128
CPL = [0, 2, 4, 6, 1, 3, 5, 7]

ENGS = ("tensor", "vector", "scalar", "sync", "gpsimd")


class Prog:

    def __init__(self):
        self.items = {e: [] for e in ENGS}
        self.counts = {"c": 0, "sync": 0, "gpsimd": 0}
        self.last = None

    def emit(self, eng, fn, dma=False):
        wait = self.last
        if dma:
            assert eng in ("sync", "gpsimd")
            self.counts[eng] += 16
            inc = (eng, 16)
            self.last = (eng, self.counts[eng])
        else:
            self.counts["c"] += 1
            inc = ("c", 1)
            self.last = ("c", self.counts["c"])
        self.items[eng].append((wait, fn, inc))

    def realize(self, nc, sems):
        with nc.Block() as blk:
            for ename in ENGS:
                if not self.items[ename]:
                    continue

                def make_body(items):
                    def body(eng):
                        for wait, fn, inc in items:
                            if wait is not None:
                                eng.wait_ge(sems[wait[0]], wait[1])
                            inst = fn(eng)
                            inst.then_inc(sems[inc[0]], inc[1])
                    return body

                getattr(blk, ename)(make_body(self.items[ename]))
            if self.last is not None:
                last = self.last

                def fence(eng):
                    eng.wait_ge(sems[last[0]], last[1])

                blk.gpsimd(fence)


class LB:

    def __init__(self):
        self.nc = bass.Bass("TRN2", target_bir_lowering=False, debug=False)
        self.p = Prog()
        self._ctx = []
        self.inputs = []
        self.outputs = []

    def dram_in(self, name, shape, dtype):
        self.inputs.append(name)
        return self.nc.dram_tensor(name, shape, dtype, kind="ExternalInput").ap()

    def dram_out(self, name, shape, dtype):
        self.outputs.append(name)
        return self.nc.dram_tensor(name, shape, dtype, kind="ExternalOutput").ap()

    def dram_scratch(self, name, shape, dtype):
        return self.nc.dram_tensor(name, shape, dtype, kind="Internal").ap()

    def sb(self, name, shape, dtype, scope=None):
        t = self.nc.sbuf_tensor(name, shape, dtype)
        h = t.__enter__()
        (self._ctx if scope is None else scope).append(t)
        return h

    def close_scope(self, scope):
        for t in reversed(scope):
            t.__exit__(None, None, None)

    def ps(self, shape, dtype):
        t = self.nc.psum_tensor(shape, dtype)
        h = t.__enter__()
        self._ctx.append(t)
        return h

    def din(self, dst, src):
        self.p.emit("sync", lambda e: e.dma_start(out=dst, in_=src), dma=True)

    def dout(self, dst, src):
        self.p.emit("gpsimd", lambda e: e.dma_start(out=dst, in_=src), dma=True)

    def mm(self, out, lhsT, rhs, start, stop, tile_position=None):
        self.p.emit("tensor", lambda e: e.matmul(
            out, lhsT, rhs, start=start, stop=stop,
            skip_group_check=True, tile_position=tile_position))

    def trans(self, out, in_, ident):
        self.p.emit("tensor", lambda e: e.transpose(out, in_, ident))

    def cp(self, out, in_, eng="vector"):
        if eng == "scalar":
            self.p.emit(eng, lambda e: e.copy(out, in_))
        else:
            self.p.emit(eng, lambda e: e.tensor_copy(out, in_))

    def tt(self, out, a, b, op, eng="vector"):
        self.p.emit(eng, lambda e: e.tensor_tensor(out, a, b, op))

    def ts(self, out, in0, s1, s2, op0, op1=None, eng="vector"):
        if op1 is None:
            self.p.emit(eng, lambda e: e.tensor_scalar(out, in0, s1, None, op0))
        else:
            self.p.emit(eng, lambda e: e.tensor_scalar(out, in0, s1, s2, op0, op1))

    def stt(self, out, in0, scalar, in1, op0, op1, eng="vector"):
        self.p.emit(eng, lambda e: e.scalar_tensor_tensor(
            out=out, in0=in0, scalar=scalar, in1=in1, op0=op0, op1=op1))

    def act(self, out, in_, func, bias=None, scale=None, accum_out=None):
        kw = {}
        if bias is not None:
            kw["bias"] = bias
        if scale is not None:
            kw["scale"] = scale
        if accum_out is not None:
            kw["accum_out"] = accum_out
        self.p.emit("scalar", lambda e: e.activation(out, in_, func, **kw))

    def memset(self, ap, val, eng="vector"):
        self.p.emit(eng, lambda e: e.memset(ap, val))

    def iota(self, out, pattern, base=None, channel_multiplier=None):
        self.p.emit("gpsimd", lambda e: e.iota(
            out, pattern, base=base, channel_multiplier=channel_multiplier,
            allow_small_or_imprecise_dtypes=True))

    def recip(self, out, in_):
        self.p.emit("vector", lambda e: e.reciprocal(out, in_))

    def rsum(self, out, in_):
        self.p.emit("vector", lambda e: e.reduce_sum(
            out, in_, axis=mybir.AxisListType.X))

    def finish(self):
        sems = {}
        for k in ("c", "sync", "gpsimd"):
            t = self.nc.semaphore("sem_" + k)
            sems[k] = t.__enter__()
            self._ctx.append(t)
        self.p.realize(self.nc, sems)
        for t in reversed(self._ctx):
            t.__exit__(None, None, None)
        return self.nc


def chunked_w_dma(lb, dst_sb, w_hbm_cols, nchunk=DC, groups=4):
    view = w_hbm_cols.rearrange("(c p) n -> p c n", p=128)
    step = nchunk // groups
    for gidx in range(groups):
        lb.din(dst_sb[:, gidx * step:(gidx + 1) * step, :],
               view[:, gidx * step:(gidx + 1) * step, :])


def emit_identity(lb, ident, iot):
    lb.iota(iot[:], [[1, 128]], channel_multiplier=-1)
    lb.ts(ident[:], iot[:], 0.0, None, ALU.is_equal)


def emit_ln(lb, x_hbm, xn_sb, xn_hbm, psb, xin, sqf, rows, bcast, onesf):
    ntb = TPC // 512
    ones_col = onesf[:, 0:1]
    ones_row = onesf[0:1, :]
    for c in range(DC):
        lb.din(xin[:], x_hbm[c * 128:(c + 1) * 128, :])
        lb.act(sqf[:], xin[:], AF.Square)
        for tb in range(ntb):
            lb.mm(psb[0:1, tb * 512:(tb + 1) * 512], ones_col,
                  xin[:, tb * 512:(tb + 1) * 512], start=(c == 0), stop=(c == DC - 1))
            lb.mm(psb[0:1, 1024 + tb * 512:1024 + (tb + 1) * 512], ones_col,
                  sqf[:, tb * 512:(tb + 1) * 512], start=(c == 0), stop=(c == DC - 1))
    mrow = rows[:, 0:TPC]
    vrow = rows[:, TPC:2 * TPC]
    srow = rows[:, 2 * TPC:3 * TPC]
    rrow = rows[:, 3 * TPC:4 * TPC]
    tmp = rows[:, 4 * TPC:5 * TPC]
    mrs = rows[:, 5 * TPC:6 * TPC]
    lb.ts(mrow, psb[0:1, 0:TPC], 1.0 / D, None, ALU.mult)
    lb.ts(vrow, psb[0:1, 1024:1024 + TPC], 1.0 / D, None, ALU.mult)
    lb.tt(tmp, mrow, mrow, ALU.mult)
    lb.tt(vrow, vrow, tmp, ALU.subtract)
    eps = rows[:, 6 * TPC:6 * TPC + 1]
    lb.memset(eps, 1e-5)
    lb.act(srow, vrow, AF.Sqrt, bias=eps, scale=1.0)
    lb.recip(rrow, srow)
    lb.tt(mrs, mrow, rrow, ALU.mult)
    rb = bcast[:, 0:TPC]
    mb = bcast[:, TPC:2 * TPC]
    for tb in range(ntb):
        sl = slice(tb * 512, (tb + 1) * 512)
        lb.mm(psb[:, 2048 + tb * 512:2048 + (tb + 1) * 512], ones_row, rrow[:, sl],
              start=True, stop=True)
        lb.cp(rb[:, sl], psb[:, 2048 + tb * 512:2048 + (tb + 1) * 512])
        lb.mm(psb[:, 2048 + tb * 512:2048 + (tb + 1) * 512], ones_row, mrs[:, sl],
              start=True, stop=True)
        lb.cp(mb[:, sl], psb[:, 2048 + tb * 512:2048 + (tb + 1) * 512])
    for c in range(DC):
        lb.din(xin[:], x_hbm[c * 128:(c + 1) * 128, :])
        lb.tt(sqf[:], xin[:], rb, ALU.mult)
        lb.tt(xn_sb[:, c, :], sqf[:], mb, ALU.subtract)
        if xn_hbm is not None:
            lb.dout(xn_hbm[c * 128:(c + 1) * 128, :], xn_sb[:, c, :])


def emit_proj(lb, w_hbm, bias_sb, nb, xn_sb, psb, wf32, wbf, stage, out_hbm,
              nblocks_free=None):
    ntb = TPC // 512
    chunked_w_dma(lb, wf32, w_hbm[:, nb * 128:(nb + 1) * 128])
    lb.cp(wbf[:, :, :], wf32[:, :, :])
    for tb in range(ntb):
        for c in range(DC):
            lb.mm(psb[:, tb * 512:(tb + 1) * 512], wbf[:, c, :],
                  xn_sb[:, c, tb * 512:(tb + 1) * 512],
                  start=(c == 0), stop=(c == DC - 1))
    for tb in range(ntb):
        sl = slice(tb * 512, (tb + 1) * 512)
        if bias_sb is not None:
            lb.act(stage[:, sl], psb[:, sl], AF.Identity,
                   bias=bias_sb, scale=1.0)
        else:
            lb.cp(stage[:, sl], psb[:, sl], eng="scalar")
    if out_hbm is not None:
        lb.dout(out_hbm[nb * 128:(nb + 1) * 128, :], stage[:])


def build_l0pre():
    lb = LB()
    xT = lb.dram_in("xT", [D, TPC], F32)
    wqT = lb.dram_in("wqT", [D, D], F32)
    wqb = lb.dram_in("wqb", [128, DC], F32)
    wkT = lb.dram_in("wkT", [D, 128], F32)
    wkb = lb.dram_in("wkb", [128, 1], F32)
    wvT = lb.dram_in("wvT", [D, 128], F32)
    wvb = lb.dram_in("wvb", [128, 1], F32)
    xnT = lb.dram_out("xnT", [D, TPC], BF16)
    qT = lb.dram_out("qT", [D, TPC], BF16)
    kT = lb.dram_out("kT", [128, TPC], BF16)
    vn = lb.dram_out("vn", [TPC, 128], BF16)

    psb = lb.ps([128, 3584], F32)
    pst = lb.ps([128, 1024], BF16)
    xin = lb.sb("xin", [128, TPC], F32)
    sqf = lb.sb("sqf", [128, TPC], F32)
    rows = lb.sb("rows", [1, 6 * TPC + 8], F32)
    bcast = lb.sb("bcast", [128, 2 * TPC], F32)
    onesf = lb.sb("onesf", [128, 128], F32)
    xnsb = lb.sb("xnsb", [128, DC, TPC], BF16)
    wf32 = lb.sb("wf32", [128, DC, 128], F32)
    wbf = lb.sb("wbf", [128, DC, 128], BF16)
    stage = lb.sb("stage", [128, TPC], BF16)
    bcol = lb.sb("bcol", [128, DC], F32)
    kvb = lb.sb("kvb", [128, 2], F32)
    ident = lb.sb("ident", [128, 128], BF16)
    iot = lb.sb("iot", [128, 128], I32)
    vns = lb.sb("vns", [128, 128], BF16)

    lb.memset(onesf[:], 1.0)
    emit_identity(lb, ident, iot)
    lb.din(bcol[:], wqb)
    lb.din(kvb[:, 0:1], wkb)
    lb.din(kvb[:, 1:2], wvb)

    emit_ln(lb, xT, xnsb, xnT, psb, xin, sqf, rows, bcast, onesf)

    for nb in range(DC):
        emit_proj(lb, wqT, bcol[:, nb:nb + 1], nb, xnsb, psb, wf32, wbf,
                  stage, qT)
    emit_proj(lb, wkT, kvb[:, 0:1], 0, xnsb, psb, wf32[:, :, :], wbf, stage, kT)
    emit_proj(lb, wvT, kvb[:, 1:2], 0, xnsb, psb, wf32, wbf, stage, None)
    for t8 in range(TPC // 128):
        lb.trans(pst[:, 0:128], stage[:, t8 * 128:(t8 + 1) * 128], ident[:])
        lb.cp(vns[:], pst[:, 0:128])
        lb.dout(vn[t8 * 128:(t8 + 1) * 128, :], vns[:])
    return lb


def build_attn(npairs, dh, scale):
    lb = LB()
    nit = T // 128
    dhc = dh // 128
    q8 = lb.dram_in("q8", [npairs, dh, T], BF16)
    k8 = lb.dram_in("k8", [npairs, dh, T], BF16)
    v8 = lb.dram_in("v8", [npairs, T, dh], BF16)
    dmask = lb.dram_in("dmask", [128, 128], F32)
    o8 = lb.dram_out("o8", [npairs, dh, T], BF16)
    r8 = lb.dram_out("r8", [npairs, 128, nit], F32)

    psb = lb.ps([128, 3072], F32)
    pso = lb.ps([128, 512], F32)
    pst = lb.ps([128, 1024], BF16)
    qp = lb.sb("qp", [128, dhc, T], BF16)
    kp = lb.sb("kp", [128, dhc, T], BF16)
    vp = lb.sb("vp", [128, nit, dh], BF16)
    Pslab = lb.sb("Pslab", [128, T], BF16)
    msk = lb.sb("msk", [128, 128], F32)
    rsp = lb.sb("rsp", [128, 8], F32)
    rpass = lb.sb("rpass", [128, 1], F32)
    rtile = lb.sb("rtile", [128, nit], F32)
    ptsb = lb.sb("ptsb", [128, 128], BF16)
    ostage = lb.sb("ostage", [128, dhc, 128], BF16)
    ident = lb.sb("ident", [128, 128], BF16)
    iot = lb.sb("iot", [128, 128], I32)

    emit_identity(lb, ident, iot)
    lb.din(msk[:], dmask)
    for p in range(npairs):
        lb.din(qp[:], q8[p].rearrange("(c p) n -> p c n", p=128))
        lb.din(kp[:], k8[p].rearrange("(c p) n -> p c n", p=128))
        for gr in range(4):
            st = nit // 4
            lb.din(vp[:, gr * st:(gr + 1) * st, :],
                   v8[p].rearrange("(c p) n -> p c n", p=128)[:, gr * st:(gr + 1) * st, :])
        for it in range(nit):
            span = (it + 1) * 128
            npass = (span + 3071) // 3072
            for jp in range(npass):
                j0 = jp * 3072
                w = min(3072, span - j0)
                nbk = (w + 511) // 512
                for b in range(nbk):
                    cw = min(512, w - b * 512)
                    for dc_ in range(dhc):
                        lb.mm(psb[:, b * 512:b * 512 + cw],
                              qp[:, dc_, it * 128:(it + 1) * 128],
                              kp[:, dc_, j0 + b * 512:j0 + b * 512 + cw],
                              start=(dc_ == 0), stop=(dc_ == dhc - 1))
                if jp == npass - 1:
                    off = span - 128 - j0
                    lb.tt(psb[:, off:off + 128], psb[:, off:off + 128], msk[:],
                          ALU.add)
                for b in range(nbk):
                    cw = min(512, w - b * 512)
                    lb.act(Pslab[:, j0 + b * 512:j0 + b * 512 + cw],
                           psb[:, b * 512:b * 512 + cw], AF.Exp, scale=scale,
                           accum_out=rsp[:, b:b + 1])
                if nbk > 1:
                    lb.rsum(rpass[:], rsp[:, 0:nbk])
                else:
                    lb.cp(rpass[:], rsp[:, 0:1])
                if jp == 0:
                    lb.cp(rtile[:, it:it + 1], rpass[:])
                else:
                    lb.tt(rtile[:, it:it + 1], rtile[:, it:it + 1], rpass[:],
                          ALU.add)
            for jc in range(it + 1):
                lb.trans(pst[:, 0:128], Pslab[:, jc * 128:(jc + 1) * 128],
                         ident[:])
                lb.cp(ptsb[:], pst[:, 0:128])
                for dc_ in range(dhc):
                    lb.mm(pso[:, dc_ * 128:(dc_ + 1) * 128],
                          vp[:, jc, dc_ * 128:(dc_ + 1) * 128],
                          ptsb[:],
                          start=(jc == 0 and dc_ == 0),
                          stop=(jc == it and dc_ == dhc - 1))
            lb.cp(ostage.rearrange("p c n -> p (c n)"), pso[:, 0:dhc * 128],
                  eng="scalar")
            lb.dout(o8[p].rearrange("(c p) n -> p c n", p=128)[:, :, it * 128:(it + 1) * 128],
                    ostage[:])
        lb.dout(r8[p], rtile[:])
    return lb


def build_l0post():
    lb = LB()
    oT = lb.dram_in("oT", [D, TPC], BF16)
    rc = lb.dram_in("rc", [32, TPC], F32)
    xT = lb.dram_in("xT", [D, TPC], F32)
    xnT = lb.dram_in("xnT", [D, TPC], BF16)
    woT = lb.dram_in("woT", [D, D], F32)
    wob = lb.dram_in("wob", [128, DC], F32)
    W0T = lb.dram_in("W0T", [D, D], F32)
    b0c = lb.dram_in("b0c", [128, DC], F32)
    Apk = lb.dram_in("Apk", [D, 128], F32)
    Bpk = lb.dram_in("Bpk", [128, D], F32)
    idxT = lb.dram_in("idxT", [D, TPC], U8)
    x1T = lb.dram_out("x1T", [D, TPC], F32)

    scr_r = lb.dram_scratch("scr_r", [32, TPC], F32)

    psb = lb.ps([128, 3584], F32)
    bufA = lb.sb("bufA", [128, DC, TPC], BF16)
    bufB = lb.sb("bufB", [128, DC, TPC], BF16)
    rsb = lb.sb("rsb", [32, TPC], F32)
    rrow = lb.sb("rrow", [1, TPC], F32)
    rbb = lb.sb("rbb", [128, TPC], F32)
    onesf = lb.sb("onesf", [128, 128], F32)
    osb = lb.sb("osb", [128, TPC], BF16)
    wf32 = lb.sb("wf32", [128, DC, 128], F32)
    wbf = lb.sb("wbf", [128, DC, 128], BF16)
    stage = lb.sb("stage", [128, TPC], BF16)
    bcol = lb.sb("bcol", [128, 2 * DC], F32)
    hsb = lb.sb("hsb", [128, TPC], BF16)
    Bp32 = lb.sb("Bp32", [128, 128], F32)
    Bpb = lb.sb("Bpb", [128, 128], BF16)
    idxsb = lb.sb("idxsb", [128, TPC], U8)
    mk = lb.sb("mk", [128, 512], BF16)
    accf = lb.sb("accf", [128, 512], F32)
    tmpy = lb.sb("tmpy", [128, 512], F32)
    uf = lb.sb("uf", [128, 512], F32)
    gb = lb.sb("gb", [128, 512], BF16)
    sgf = lb.sb("sgf", [128, 512], F32)
    xin = lb.sb("xin", [128, TPC], F32)
    x1s = lb.sb("x1s", [128, TPC], F32)

    lb.memset(onesf[:], 1.0)
    lb.din(bcol[:, 0:DC], wob)
    lb.din(bcol[:, DC:2 * DC], b0c)

    lb.din(rsb[:], rc)
    lb.recip(rsb[:], rsb[:])
    lb.dout(scr_r, rsb[:])
    ntb = TPC // 512
    for h in range(32):
        lb.din(rrow[:], scr_r[h:h + 1, :])
        for tb in range(ntb):
            sl = slice(tb * 512, (tb + 1) * 512)
            lb.mm(psb[:, tb * 512:(tb + 1) * 512], onesf[0:1, :], rrow[:, sl],
                  start=True, stop=True)
            lb.cp(rbb[:, sl], psb[:, tb * 512:(tb + 1) * 512])
        lb.din(osb[:], oT[h * 128:(h + 1) * 128, :])
        lb.tt(bufA[:, h, :], osb[:], rbb[:], ALU.mult)

    for nb in range(DC):
        emit_proj(lb, woT, bcol[:, nb:nb + 1], nb, bufA, psb, wf32, wbf,
                  stage, None)
        lb.cp(bufB[:, nb, :], stage[:])

    for c in range(DC):
        lb.din(bufA[:, c, :], xnT[c * 128:(c + 1) * 128, :])

    chunked_w_dma(lb, wf32, Apk)
    lb.cp(wbf[:, :, :], wf32[:, :, :])
    for tb in range(ntb):
        for c in range(DC):
            lb.mm(psb[:, tb * 512:(tb + 1) * 512], wbf[:, c, :],
                  bufA[:, c, tb * 512:(tb + 1) * 512],
                  start=(c == 0), stop=(c == DC - 1))
    for tb in range(ntb):
        sl = slice(tb * 512, (tb + 1) * 512)
        lb.cp(hsb[:, sl], psb[:, sl], eng="scalar")

    for ob in range(DC):
        chunked_w_dma(lb, wf32, W0T[:, ob * 128:(ob + 1) * 128])
        lb.cp(wbf[:, :, :], wf32[:, :, :])
        lb.din(idxsb[:], idxT[ob * 128:(ob + 1) * 128, :])
        lb.din(Bp32[:], Bpk[:, ob * 128:(ob + 1) * 128])
        lb.cp(Bpb[:], Bp32[:])
        lb.din(xin[:], xT[ob * 128:(ob + 1) * 128, :])
        for tb in range(ntb):
            sl = slice(tb * 512, (tb + 1) * 512)
            for c in range(DC):
                lb.mm(psb[:, 0:512], wbf[:, c, :], bufA[:, c, sl],
                      start=(c == 0), stop=(c == DC - 1))
            for l in range(4):
                tp = (96, 0) if l == 3 else None
                lb.mm(psb[:, 512 * (1 + l):512 * (1 + l) + 512],
                      Bpb[32 * l:32 * l + 16, :], hsb[32 * l:32 * l + 16, sl],
                      start=True, stop=True, tile_position=tp)
            for l in range(4):
                lb.ts(mk[:], idxsb[:, sl], float(l), None, ALU.is_equal)
                if l == 0:
                    lb.tt(accf[:], psb[:, 512:1024], mk[:], ALU.mult)
                else:
                    lb.tt(tmpy[:], psb[:, 512 * (1 + l):512 * (1 + l) + 512],
                          mk[:], ALU.mult)
                    lb.tt(accf[:], accf[:], tmpy[:], ALU.add)
            lb.stt(uf[:], accf[:], 8.0, psb[:, 0:512], ALU.mult, ALU.add)
            lb.act(gb[:], uf[:], AF.Silu, bias=bcol[:, DC + ob:DC + ob + 1],
                   scale=1.0)
            lb.tt(sgf[:], bufB[:, ob, sl], gb[:], ALU.mult)
            lb.tt(x1s[:, sl], xin[:, sl], sgf[:], ALU.add)
        lb.dout(x1T[ob * 128:(ob + 1) * 128, :], x1s[:])
    return lb


def build_l1pre():
    lb = LB()
    x1T = lb.dram_in("x1T", [D, TPC], F32)
    Wq = lb.dram_in("Wq", [D, D], F32)
    Wk = lb.dram_in("Wk", [D, D], F32)
    Wv = lb.dram_in("Wv", [D, D], F32)
    cosc = lb.dram_in("cosc", [64, TPC], F32)
    sinc = lb.dram_in("sinc", [64, TPC], F32)
    xn1T = lb.dram_out("xn1T", [D, TPC], BF16)
    q1T = lb.dram_out("q1T", [D, TPC], BF16)
    k1T = lb.dram_out("k1T", [D, TPC], BF16)
    v1n = lb.dram_out("v1n", [TPC, D], BF16)

    psb = lb.ps([128, 3584], F32)
    pst = lb.ps([128, 1024], BF16)
    xin = lb.sb("xin", [128, TPC], F32)
    sqf = lb.sb("sqf", [128, TPC], F32)
    rows = lb.sb("rows", [1, 6 * TPC + 8], F32)
    bcast = lb.sb("bcast", [128, 2 * TPC], F32)
    onesf = lb.sb("onesf", [128, 128], F32)
    xnsb = lb.sb("xnsb", [128, DC, TPC], BF16)
    wf32 = lb.sb("wf32", [128, DC, 128], F32)
    wbf = lb.sb("wbf", [128, DC, 128], BF16)
    stage = lb.sb("stage", [128, TPC], BF16)
    cs = lb.sb("cs", [64, 2 * TPC], F32)
    qrb = lb.sb("qrb", [64, TPC], BF16)
    qsw = lb.sb("qsw", [64, TPC], BF16)
    ta = lb.sb("ta", [64, TPC], BF16)
    tb2 = lb.sb("tb2", [64, TPC], BF16)
    ident = lb.sb("ident", [128, 128], BF16)
    iot = lb.sb("iot", [128, 128], I32)
    vns = lb.sb("vns", [128, 128], BF16)

    lb.memset(onesf[:], 1.0)
    emit_identity(lb, ident, iot)
    lb.din(cs[:, 0:TPC], cosc)
    lb.din(cs[:, TPC:2 * TPC], sinc)

    emit_ln(lb, x1T, xnsb, xn1T, psb, xin, sqf, rows, bcast, onesf)

    ntb = TPC // 512

    def proj_out(w_hbm, dst_hbm):
        for nb in range(DC):
            chunked_w_dma(lb, wf32, w_hbm[:, nb * 128:(nb + 1) * 128])
            lb.cp(wbf[:, :, :], wf32[:, :, :])
            for tb in range(ntb):
                for c in range(DC):
                    lb.mm(psb[:, tb * 512:(tb + 1) * 512], wbf[:, c, :],
                          xnsb[:, c, tb * 512:(tb + 1) * 512],
                          start=(c == 0), stop=(c == DC - 1))
            for tb in range(ntb):
                sl = slice(tb * 512, (tb + 1) * 512)
                lb.cp(stage[:, sl], psb[:, sl], eng="scalar")
            lb.dout(dst_hbm[nb * 128:(nb + 1) * 128, :], stage[:])

    def rotary(dst_hbm):
        cosr = cs[:, 0:TPC]
        sinr = cs[:, TPC:2 * TPC]
        for h in range(16):
            r0 = 256 * h
            lb.din(qrb[:], dst_hbm[r0:r0 + 64, :])
            lb.din(qsw[0:32, :], qrb[32:64, :])
            lb.din(qsw[32:64, :], qrb[0:32, :])
            lb.tt(ta[:], qrb[:], cosr, ALU.mult)
            lb.tt(tb2[:], qsw[:], sinr, ALU.mult)
            lb.tt(qrb[:], ta[:], tb2[:], ALU.add)
            lb.dout(dst_hbm[r0:r0 + 64, :], qrb[:])

    proj_out(Wq, q1T)
    rotary(q1T)
    proj_out(Wk, k1T)
    rotary(k1T)
    for nb in range(DC):
        chunked_w_dma(lb, wf32, Wv[:, nb * 128:(nb + 1) * 128])
        lb.cp(wbf[:, :, :], wf32[:, :, :])
        for tb in range(ntb):
            for c in range(DC):
                lb.mm(psb[:, tb * 512:(tb + 1) * 512], wbf[:, c, :],
                      xnsb[:, c, tb * 512:(tb + 1) * 512],
                      start=(c == 0), stop=(c == DC - 1))
        for tb in range(ntb):
            sl = slice(tb * 512, (tb + 1) * 512)
            lb.cp(stage[:, sl], psb[:, sl], eng="scalar")
        for t8 in range(TPC // 128):
            lb.trans(pst[:, 0:128], stage[:, t8 * 128:(t8 + 1) * 128], ident[:])
            lb.cp(vns[:], pst[:, 0:128])
            lb.dout(v1n[t8 * 128:(t8 + 1) * 128, nb * 128:(nb + 1) * 128],
                    vns[:])
    return lb


NI = 11008


def build_l1post():
    lb = LB()
    oT = lb.dram_in("oT", [D, TPC], BF16)
    rc = lb.dram_in("rc", [16, TPC], F32)
    x1T = lb.dram_in("x1T", [D, TPC], F32)
    xn1T = lb.dram_in("xn1T", [D, TPC], BF16)
    Wo = lb.dram_in("Wo", [D, D], F32)
    gqw = lb.dram_in("gqw", [D, NI // 8], I32)
    gsc = lb.dram_in("gsc", [32, NI], F32)
    gqz = lb.dram_in("gqz", [32, NI // 8], I32)
    uqw = lb.dram_in("uqw", [D, NI // 8], I32)
    usc = lb.dram_in("usc", [32, NI], F32)
    uqz = lb.dram_in("uqz", [32, NI // 8], I32)
    dqw = lb.dram_in("dqw", [NI, D // 8], I32)
    dsc = lb.dram_in("dsc", [86, D], F32)
    dqz = lb.dram_in("dqz", [86, D // 8], I32)
    outT = lb.dram_out("outT", [D, TPC], F32)

    scr_r = lb.dram_scratch("scr_r", [16, TPC], F32)
    zr_g = lb.dram_scratch("zr_g", [32, NI], BF16)
    sr_g = lb.dram_scratch("sr_g", [32, NI], BF16)
    zr_u = lb.dram_scratch("zr_u", [32, NI], BF16)
    sr_u = lb.dram_scratch("sr_u", [32, NI], BF16)
    zr_d = lb.dram_scratch("zr_d", [86, D], BF16)
    sr_d = lb.dram_scratch("sr_d", [86, D], BF16)
    dq_g = lb.dram_scratch("dq_g", [D, NI], BF16)
    dq_u = lb.dram_scratch("dq_u", [D, NI], BF16)
    dq_d = lb.dram_scratch("dq_d", [NI, D], BF16)
    hTs = lb.dram_scratch("hTs", [NI, TPC], BF16)

    psb = lb.ps([128, 4096], F32)
    ntb = TPC // 512

    dq_scope = []
    qwsb = lb.sb("qwsb", [128, 344], I32, scope=dq_scope)
    nibsb = lb.sb("nibsb", [128, 2752], I32, scope=dq_scope)
    t1b = lb.sb("t1b", [128, 2752], BF16, scope=dq_scope)
    ssbc = lb.sb("ssbc", [128, 2752], F32, scope=dq_scope)
    zbc = lb.sb("zbc", [128, 2752], BF16, scope=dq_scope)
    sbc = lb.sb("sbc", [128, 2752], BF16, scope=dq_scope)
    wrowc = lb.sb("wrowc", [128, 2752], BF16, scope=dq_scope)

    def unpack8(dst_i32, src_i32, chw):
        for j in range(8):
            dst = dst_i32[:, CPL[j]:chw:8]
            if j == 0:
                lb.ts(dst, src_i32, 15, None, ALU.bitwise_and)
            else:
                lb.ts(dst, src_i32, 4 * j, 15,
                      ALU.logical_shift_right, ALU.bitwise_and)

    def dequant(qw, qz, sc, zr, sr, dq, G, N):
        chw = 2752 if N % 2752 == 0 else 2048
        nch = N // chw
        for ch in range(nch):
            c0 = ch * chw
            lb.din(qwsb[:G, 0:chw // 8], qz[:, c0 // 8:(c0 + chw) // 8])
            unpack8(nibsb[:G, 0:chw], qwsb[:G, 0:chw // 8], chw)
            lb.cp(t1b[:G, 0:chw], nibsb[:G, 0:chw])
            lb.dout(zr[:, c0:c0 + chw], t1b[:G, 0:chw])
            lb.din(ssbc[:G, 0:chw], sc[:, c0:c0 + chw])
            lb.cp(t1b[:G, 0:chw], ssbc[:G, 0:chw])
            lb.dout(sr[:, c0:c0 + chw], t1b[:G, 0:chw])
        for g in range(G):
            for ch in range(nch):
                c0 = ch * chw
                lb.din(zbc[:, 0:chw], zr[g:g + 1, c0:c0 + chw].broadcast_to([128, chw]))
                lb.din(sbc[:, 0:chw], sr[g:g + 1, c0:c0 + chw].broadcast_to([128, chw]))
                lb.din(qwsb[:, 0:chw // 8],
                       qw[g * 128:(g + 1) * 128, c0 // 8:(c0 + chw) // 8])
                unpack8(nibsb[:, 0:chw], qwsb[:, 0:chw // 8], chw)
                lb.tt(t1b[:, 0:chw], nibsb[:, 0:chw], zbc[:, 0:chw], ALU.subtract)
                lb.tt(wrowc[:, 0:chw], t1b[:, 0:chw], sbc[:, 0:chw], ALU.mult)
                lb.dout(dq[g * 128:(g + 1) * 128, c0:c0 + chw], wrowc[:, 0:chw])

    dequant(gqw, gqz, gsc, zr_g, sr_g, dq_g, 32, NI)
    dequant(uqw, uqz, usc, zr_u, sr_u, dq_u, 32, NI)
    dequant(dqw, dqz, dsc, zr_d, sr_d, dq_d, 86, D)
    lb.close_scope(dq_scope)

    bufA = lb.sb("bufA", [128, DC, TPC], BF16)
    bufB = lb.sb("bufB", [128, DC, TPC], BF16)
    rsb = lb.sb("rsb", [16, TPC], F32)
    rrow = lb.sb("rrow", [1, TPC], F32)
    rbb = lb.sb("rbb", [128, TPC], F32)
    onesf = lb.sb("onesf", [128, 128], F32)
    osb = lb.sb("osb", [128, TPC], BF16)
    wf32 = lb.sb("wf32", [128, DC, 128], F32)
    wbf = lb.sb("wbf", [128, DC, 128], BF16)
    gsl = lb.sb("gsl", [128, DC, 128], BF16)
    usl = lb.sb("usl", [128, DC, 128], BF16)
    gsil = lb.sb("gsil", [128, TPC], BF16)
    hrow = lb.sb("hrow", [128, TPC], BF16)
    hsl = lb.sb("hsl", [128, TPC], BF16)
    dsl = lb.sb("dsl", [128, 512], BF16)
    gf = lb.sb("gf", [128, 512], F32)
    sgf = lb.sb("sgf", [128, 512], F32)
    xin = lb.sb("xin", [128, TPC], F32)
    x1s = lb.sb("x1s", [128, TPC], F32)

    lb.memset(onesf[:], 1.0)

    lb.din(rsb[:], rc)
    lb.recip(rsb[:], rsb[:])
    lb.dout(scr_r, rsb[:])
    for h in range(16):
        lb.din(rrow[:], scr_r[h:h + 1, :])
        for tb in range(ntb):
            sl = slice(tb * 512, (tb + 1) * 512)
            lb.mm(psb[:, tb * 512:(tb + 1) * 512], onesf[0:1, :], rrow[:, sl],
                  start=True, stop=True)
            lb.cp(rbb[:, sl], psb[:, tb * 512:(tb + 1) * 512])
        for c2 in range(2):
            c = 2 * h + c2
            lb.din(osb[:], oT[c * 128:(c + 1) * 128, :])
            lb.tt(bufA[:, c, :], osb[:], rbb[:], ALU.mult)

    for nb in range(DC):
        chunked_w_dma(lb, wf32, Wo[:, nb * 128:(nb + 1) * 128])
        lb.cp(wbf[:, :, :], wf32[:, :, :])
        for tb in range(ntb):
            for c in range(DC):
                lb.mm(psb[:, tb * 512:(tb + 1) * 512], wbf[:, c, :],
                      bufA[:, c, tb * 512:(tb + 1) * 512],
                      start=(c == 0), stop=(c == DC - 1))
        for tb in range(ntb):
            sl = slice(tb * 512, (tb + 1) * 512)
            lb.cp(bufB[:, nb, sl], psb[:, sl], eng="scalar")

    for c in range(DC):
        lb.din(bufA[:, c, :], xn1T[c * 128:(c + 1) * 128, :])
    NIB = NI // 128
    for Ib in range(NIB):
        gview = dq_g[:, Ib * 128:(Ib + 1) * 128].rearrange("(c p) n -> p c n", p=128)
        uview = dq_u[:, Ib * 128:(Ib + 1) * 128].rearrange("(c p) n -> p c n", p=128)
        for gr in range(4):
            st = DC // 4
            lb.din(gsl[:, gr * st:(gr + 1) * st, :], gview[:, gr * st:(gr + 1) * st, :])
            lb.din(usl[:, gr * st:(gr + 1) * st, :], uview[:, gr * st:(gr + 1) * st, :])
        for tb in range(ntb):
            for c in range(DC):
                lb.mm(psb[:, tb * 512:(tb + 1) * 512], gsl[:, c, :],
                      bufA[:, c, tb * 512:(tb + 1) * 512],
                      start=(c == 0), stop=(c == DC - 1))
                lb.mm(psb[:, 1024 + tb * 512:1024 + (tb + 1) * 512],
                      usl[:, c, :], bufA[:, c, tb * 512:(tb + 1) * 512],
                      start=(c == 0), stop=(c == DC - 1))
        for tb in range(ntb):
            sl = slice(tb * 512, (tb + 1) * 512)
            lb.act(gsil[:, sl], psb[:, sl], AF.Silu)
            lb.tt(hrow[:, sl], psb[:, 1024 + tb * 512:1024 + (tb + 1) * 512],
                  gsil[:, sl], ALU.mult)
        lb.dout(hTs[Ib * 128:(Ib + 1) * 128, :], hrow[:])

    for ds in range(8):
        for g in range(NIB):
            lb.din(hsl[:], hTs[g * 128:(g + 1) * 128, :])
            lb.din(dsl[:], dq_d[g * 128:(g + 1) * 128, ds * 512:(ds + 1) * 512])
            for db in range(4):
                for tb in range(ntb):
                    lb.mm(psb[:, (db * 2 + tb) * 512:(db * 2 + tb + 1) * 512],
                          dsl[:, db * 128:(db + 1) * 128],
                          hsl[:, tb * 512:(tb + 1) * 512],
                          start=(g == 0), stop=(g == NIB - 1))
        for db in range(4):
            dblk = ds * 4 + db
            lb.din(xin[:], x1T[dblk * 128:(dblk + 1) * 128, :])
            for tb in range(ntb):
                sl = slice(tb * 512, (tb + 1) * 512)
                lb.act(gf[:], psb[:, (db * 2 + tb) * 512:(db * 2 + tb + 1) * 512],
                       AF.Silu)
                lb.tt(sgf[:], bufB[:, dblk, sl], gf[:], ALU.mult)
                lb.tt(x1s[:, sl], xin[:, sl], sgf[:], ALU.add)
            lb.dout(outT[dblk * 128:(dblk + 1) * 128, :], x1s[:])
    return lb


def _run(lb, in_maps):
    nc = lb.finish()
    res = run_bass_kernel_spmd(nc, in_maps, core_ids=list(range(NCORES)))
    return res.results


def _cont(a):
    return np.ascontiguousarray(a)


def _threefry2x32(k1, k2, x0, x1):
    U = np.uint32

    def rotl(x, d):
        return (x << U(d)) | (x >> U(32 - d))

    ks0, ks1 = U(k1), U(k2)
    ks2 = ks0 ^ ks1 ^ U(0x1BD11BDA)
    x = [x0.astype(U) + ks0, x1.astype(U) + ks1]

    def rounds(x, rots):
        for r in rots:
            x[0] = x[0] + x[1]
            x[1] = rotl(x[1], r) ^ x[0]
        return x

    r0 = [13, 15, 26, 6]
    r1 = [17, 29, 16, 24]
    x = rounds(x, r0)
    x[0] += ks1
    x[1] += ks2 + U(1)
    x = rounds(x, r1)
    x[0] += ks2
    x[1] += ks0 + U(2)
    x = rounds(x, r0)
    x[0] += ks0
    x[1] += ks1 + U(3)
    x = rounds(x, r1)
    x[0] += ks1
    x[1] += ks2 + U(4)
    x = rounds(x, r0)
    x[0] += ks2
    x[1] += ks0 + U(5)
    return x[0], x[1]


def _mask_idx():
    n = B * T * D
    counts = np.arange(n, dtype=np.uint64)
    c_hi = (counts >> np.uint64(32)).astype(np.uint32)
    c_lo = (counts & np.uint64(0xFFFFFFFF)).astype(np.uint32)
    b1, b2 = _threefry2x32(np.uint32(0), np.uint32(42), c_hi, c_lo)
    bits = b1 ^ b2
    fb = (bits >> np.uint32(9)) | np.uint32(0x3F800000)
    u = fb.view(np.float32) - np.float32(1.0)
    idx = ((u >= 0.25).astype(np.uint8) + (u >= 0.5).astype(np.uint8)
           + (u >= 0.75).astype(np.uint8))
    return idx.reshape(B, T, D)


def _rot_tables():
    inv_freq = (1.0 / 10000.0 ** (np.arange(0, 64, 2, dtype=np.float32) / 64.0)
                ).astype(np.float32)
    t = np.arange(T, dtype=np.float32)
    sinu = np.einsum("i,j->ij", t, inv_freq).astype(np.float32)
    return np.sin(sinu).astype(np.float32), np.cos(sinu).astype(np.float32)


def _l1_perm():
    perm = np.zeros(D, dtype=np.int64)
    for h in range(16):
        base = 256 * h
        newo = ([base + 2 * i for i in range(32)]
                + [base + 2 * i + 1 for i in range(32)]
                + [base + i for i in range(64, 256)])
        perm[base:base + 256] = newo
    return perm


def kernel(**kw):
    return _forward(kw, dbg=None)


def _forward(kw, dbg=None):
    (L0_attn_x, L0_attn_wq_w, L0_attn_wq_b, L0_attn_wk_w, L0_attn_wk_b,
     L0_attn_wv_w, L0_attn_wv_b, L0_attn_wo_w, L0_attn_wo_b,
     L0_ffn_W0, L0_ffn_b0, L0_ffn_A, L0_ffn_B,
     L1_attn_Wq, L1_attn_Wk, L1_attn_Wv, L1_attn_Wo,
     L1_ffn_gate_qweight, L1_ffn_gate_scales, L1_ffn_gate_qzeros,
     L1_ffn_up_qweight, L1_ffn_up_scales, L1_ffn_up_qzeros,
     L1_ffn_down_qweight, L1_ffn_down_scales, L1_ffn_down_qzeros) = (
        kw[n] for n in (
            "L0_attn_x", "L0_attn_wq_w", "L0_attn_wq_b", "L0_attn_wk_w",
            "L0_attn_wk_b", "L0_attn_wv_w", "L0_attn_wv_b", "L0_attn_wo_w",
            "L0_attn_wo_b", "L0_ffn_W0", "L0_ffn_b0", "L0_ffn_A", "L0_ffn_B",
            "L1_attn_Wq", "L1_attn_Wk", "L1_attn_Wv", "L1_attn_Wo",
            "L1_ffn_gate_qweight", "L1_ffn_gate_scales", "L1_ffn_gate_qzeros",
            "L1_ffn_up_qweight", "L1_ffn_up_scales", "L1_ffn_up_qzeros",
            "L1_ffn_down_qweight", "L1_ffn_down_scales", "L1_ffn_down_qzeros"))
    f32 = np.float32

    x = np.asarray(L0_attn_x, dtype=f32)
    xT = _cont(x.reshape(TOK, D).T)

    wqT = _cont(np.asarray(L0_attn_wq_w, f32).T)
    wqb = _cont(np.asarray(L0_attn_wq_b, f32).reshape(DC, 128).T)
    wkT = _cont(np.asarray(L0_attn_wk_w, f32).T)
    wkb = _cont(np.asarray(L0_attn_wk_b, f32)[:, None])
    wvT = _cont(np.asarray(L0_attn_wv_w, f32).T)
    wvb = _cont(np.asarray(L0_attn_wv_b, f32)[:, None])

    lb = build_l0pre()
    in_maps = []
    for c in range(NCORES):
        in_maps.append({
            "xT": _cont(xT[:, c * TPC:(c + 1) * TPC]),
            "wqT": wqT, "wqb": wqb, "wkT": wkT, "wkb": wkb,
            "wvT": wvT, "wvb": wvb,
        })
    r1 = _run(lb, in_maps)
    xnT_g = np.concatenate([r1[c]["xnT"] for c in range(NCORES)], axis=1)
    qT_g = np.concatenate([r1[c]["qT"] for c in range(NCORES)], axis=1)
    kT_g = np.concatenate([r1[c]["kT"] for c in range(NCORES)], axis=1)
    vn_g = np.concatenate([r1[c]["vn"] for c in range(NCORES)], axis=0)
    if dbg is not None:
        dbg.update(xnT=xnT_g, qT=qT_g, kT=kT_g, vn=vn_g)
        if dbg.get("stop") == 1:
            return None

    dmask = np.triu(np.full((128, 128), -1e30, dtype=f32), k=1)
    lb = build_attn(8, 128, 1.0 / math.sqrt(128.0))
    in_maps = []
    for c in range(NCORES):
        q8 = np.zeros((8, 128, T), dtype=bfdt)
        k8 = np.zeros((8, 128, T), dtype=bfdt)
        v8 = np.zeros((8, T, 128), dtype=bfdt)
        for p in range(8):
            gpair = c * 8 + p
            b, h = gpair // 32, gpair % 32
            q8[p] = qT_g[h * 128:(h + 1) * 128, b * T:(b + 1) * T]
            k8[p] = kT_g[:, b * T:(b + 1) * T]
            v8[p] = vn_g[b * T:(b + 1) * T, :]
        in_maps.append({"q8": q8, "k8": k8, "v8": v8, "dmask": dmask})
    r2 = _run(lb, in_maps)

    oT_g = np.zeros((D, TOK), dtype=bfdt)
    r_g = np.zeros((32, TOK), dtype=f32)
    for c in range(NCORES):
        for p in range(8):
            gpair = c * 8 + p
            b, h = gpair // 32, gpair % 32
            oT_g[h * 128:(h + 1) * 128, b * T:(b + 1) * T] = r2[c]["o8"][p]
            r_g[h, b * T:(b + 1) * T] = _cont(r2[c]["r8"][p].T).reshape(T)
    if dbg is not None:
        dbg.update(oT=oT_g, r=r_g)
        if dbg.get("stop") == 2:
            return None

    woT = _cont(np.asarray(L0_attn_wo_w, f32).T)
    wob = _cont(np.asarray(L0_attn_wo_b, f32).reshape(DC, 128).T)
    W0T = _cont(np.asarray(L0_ffn_W0, f32).T)
    b0c = _cont(np.asarray(L0_ffn_b0, f32).reshape(DC, 128).T)
    A = np.asarray(L0_ffn_A, f32)
    Bm = np.asarray(L0_ffn_B, f32)
    Apk = np.zeros((D, 128), dtype=f32)
    Bpk = np.zeros((128, D), dtype=f32)
    for l in range(4):
        Apk[:, 32 * l:32 * l + 16] = A[l].T
        Bpk[32 * l:32 * l + 16, :] = Bm[l].T
    idx = _mask_idx()
    idxT_g = _cont(idx.reshape(TOK, D).T)

    lb = build_l0post()
    in_maps = []
    for c in range(NCORES):
        sl = slice(c * TPC, (c + 1) * TPC)
        in_maps.append({
            "oT": _cont(oT_g[:, sl]), "rc": _cont(r_g[:, sl]),
            "xT": _cont(xT[:, sl]), "xnT": _cont(xnT_g[:, sl]),
            "woT": woT, "wob": wob, "W0T": W0T, "b0c": b0c,
            "Apk": Apk, "Bpk": Bpk, "idxT": _cont(idxT_g[:, sl]),
        })
    r3 = _run(lb, in_maps)
    x1T_g = np.concatenate([r3[c]["x1T"] for c in range(NCORES)], axis=1)
    if dbg is not None:
        dbg.update(x1T=x1T_g)
        if dbg.get("stop") == 3:
            return None

    perm = _l1_perm()
    Wq = _cont(np.asarray(L1_attn_Wq, f32)[:, perm])
    Wk = _cont(np.asarray(L1_attn_Wk, f32)[:, perm])
    Wv = _cont(np.asarray(L1_attn_Wv, f32))
    sint, cost = _rot_tables()
    cosc_g = _cont(np.tile(np.concatenate([cost.T, cost.T], axis=0), (1, B)))
    sinc_g = _cont(np.tile(np.concatenate([sint.T, sint.T], axis=0), (1, B)))

    lb = build_l1pre()
    in_maps = []
    for c in range(NCORES):
        sl = slice(c * TPC, (c + 1) * TPC)
        in_maps.append({
            "x1T": _cont(x1T_g[:, sl]), "Wq": Wq, "Wk": Wk, "Wv": Wv,
            "cosc": _cont(cosc_g[:, sl]), "sinc": _cont(sinc_g[:, sl]),
        })
    r4 = _run(lb, in_maps)
    xn1T_g = np.concatenate([r4[c]["xn1T"] for c in range(NCORES)], axis=1)
    q1T_g = np.concatenate([r4[c]["q1T"] for c in range(NCORES)], axis=1)
    k1T_g = np.concatenate([r4[c]["k1T"] for c in range(NCORES)], axis=1)
    v1n_g = np.concatenate([r4[c]["v1n"] for c in range(NCORES)], axis=0)
    if dbg is not None:
        dbg.update(xn1T=xn1T_g, q1T=q1T_g, k1T=k1T_g, v1n=v1n_g)
        if dbg.get("stop") == 4:
            return None

    lb = build_attn(4, 256, 1.0 / 16.0)
    in_maps = []
    for c in range(NCORES):
        q4 = np.zeros((4, 256, T), dtype=bfdt)
        k4 = np.zeros((4, 256, T), dtype=bfdt)
        v4 = np.zeros((4, T, 256), dtype=bfdt)
        for p in range(4):
            gpair = c * 4 + p
            b, h = gpair // 16, gpair % 16
            q4[p] = q1T_g[h * 256:(h + 1) * 256, b * T:(b + 1) * T]
            k4[p] = k1T_g[h * 256:(h + 1) * 256, b * T:(b + 1) * T]
            v4[p] = v1n_g[b * T:(b + 1) * T, h * 256:(h + 1) * 256]
        in_maps.append({"q8": q4, "k8": k4, "v8": v4, "dmask": dmask})
    r5 = _run(lb, in_maps)

    o1T_g = np.zeros((D, TOK), dtype=bfdt)
    r1_g = np.zeros((16, TOK), dtype=f32)
    for c in range(NCORES):
        for p in range(4):
            gpair = c * 4 + p
            b, h = gpair // 16, gpair % 16
            o1T_g[h * 256:(h + 1) * 256, b * T:(b + 1) * T] = r5[c]["o8"][p]
            r1_g[h, b * T:(b + 1) * T] = _cont(r5[c]["r8"][p].T).reshape(T)
    if dbg is not None:
        dbg.update(o1T=o1T_g, r1=r1_g)
        if dbg.get("stop") == 5:
            return None

    Wo1 = _cont(np.asarray(L1_attn_Wo, f32))
    lb = build_l1post()
    in_maps = []
    for c in range(NCORES):
        sl = slice(c * TPC, (c + 1) * TPC)
        in_maps.append({
            "oT": _cont(o1T_g[:, sl]), "rc": _cont(r1_g[:, sl]),
            "x1T": _cont(x1T_g[:, sl]), "xn1T": _cont(xn1T_g[:, sl]),
            "Wo": Wo1,
            "gqw": np.asarray(L1_ffn_gate_qweight, np.int32),
            "gsc": np.asarray(L1_ffn_gate_scales, f32),
            "gqz": np.asarray(L1_ffn_gate_qzeros, np.int32),
            "uqw": np.asarray(L1_ffn_up_qweight, np.int32),
            "usc": np.asarray(L1_ffn_up_scales, f32),
            "uqz": np.asarray(L1_ffn_up_qzeros, np.int32),
            "dqw": np.asarray(L1_ffn_down_qweight, np.int32),
            "dsc": np.asarray(L1_ffn_down_scales, f32),
            "dqz": np.asarray(L1_ffn_down_qzeros, np.int32),
        })
    r6 = _run(lb, in_maps)
    outT = np.concatenate([r6[c]["outT"] for c in range(NCORES)], axis=1)
    return _cont(outT.T.astype(np.float32)).reshape(B, T, D)


# version 03
# speedup vs baseline: 1.0361x; 1.0361x over previous
import math
import numpy as np
import ml_dtypes
import concourse.bass as bass
import concourse.mybir as mybir
from concourse.bass_utils import run_bass_kernel_spmd

F32 = mybir.dt.float32
BF16 = mybir.dt.bfloat16
I32 = mybir.dt.int32
U8 = mybir.dt.uint8
AF = mybir.ActivationFunctionType
ALU = mybir.AluOpType
bfdt = ml_dtypes.bfloat16

D = 4096
B = 2
T = 4096
TOK = B * T
NCORES = 8
TPC = TOK // NCORES
DC = D // 128
CPL = [0, 2, 4, 6, 1, 3, 5, 7]

ENGS = ("tensor", "vector", "scalar", "sync", "gpsimd")


class Prog:

    def __init__(self):
        self.items = {e: [] for e in ENGS}
        self.counts = {"c": 0, "sync": 0, "gpsimd": 0}
        self.last = None

    def emit(self, eng, fn, dma=False, wait="chain", link=True, extra=None):
        waits = []
        if wait == "chain":
            if self.last is not None:
                waits.append(self.last)
        elif wait is not None:
            waits.append(wait)
        if extra is not None:
            waits.append(extra)
        if dma:
            assert eng in ("sync", "gpsimd")
            self.counts[eng] += 16
            inc = (eng, 16)
            tail = (eng, self.counts[eng])
        else:
            self.counts["c"] += 1
            inc = ("c", 1)
            tail = ("c", self.counts["c"])
        if link:
            self.last = tail
        self.items[eng].append((waits, fn, inc))
        return tail

    def realize(self, nc, sems):
        with nc.Block() as blk:
            for ename in ENGS:
                if not self.items[ename]:
                    continue

                def make_body(items):
                    def body(eng):
                        for waits, fn, inc in items:
                            for w in waits:
                                eng.wait_ge(sems[w[0]], w[1])
                            inst = fn(eng)
                            inst.then_inc(sems[inc[0]], inc[1])
                    return body

                getattr(blk, ename)(make_body(self.items[ename]))
            if self.last is not None:
                last = self.last

                def fence(eng):
                    eng.wait_ge(sems[last[0]], last[1])

                blk.gpsimd(fence)


class LB:

    def __init__(self):
        self.nc = bass.Bass("TRN2", target_bir_lowering=False, debug=False)
        self.p = Prog()
        self._ctx = []
        self.inputs = []
        self.outputs = []

    def dram_in(self, name, shape, dtype):
        self.inputs.append(name)
        return self.nc.dram_tensor(name, shape, dtype, kind="ExternalInput").ap()

    def dram_out(self, name, shape, dtype):
        self.outputs.append(name)
        return self.nc.dram_tensor(name, shape, dtype, kind="ExternalOutput").ap()

    def dram_scratch(self, name, shape, dtype):
        return self.nc.dram_tensor(name, shape, dtype, kind="Internal").ap()

    def sb(self, name, shape, dtype, scope=None):
        t = self.nc.sbuf_tensor(name, shape, dtype)
        h = t.__enter__()
        (self._ctx if scope is None else scope).append(t)
        return h

    def close_scope(self, scope):
        for t in reversed(scope):
            t.__exit__(None, None, None)

    def ps(self, shape, dtype):
        t = self.nc.psum_tensor(shape, dtype)
        h = t.__enter__()
        self._ctx.append(t)
        return h

    def din(self, dst, src, **kw):
        return self.p.emit("sync", lambda e: e.dma_start(out=dst, in_=src),
                           dma=True, **kw)

    def dout(self, dst, src, **kw):
        return self.p.emit("gpsimd", lambda e: e.dma_start(out=dst, in_=src),
                           dma=True, **kw)

    def mm(self, out, lhsT, rhs, start, stop, tile_position=None, **kw):
        return self.p.emit("tensor", lambda e: e.matmul(
            out, lhsT, rhs, start=start, stop=stop,
            skip_group_check=True, tile_position=tile_position), **kw)

    def trans(self, out, in_, ident):
        self.p.emit("tensor", lambda e: e.transpose(out, in_, ident))

    def cp(self, out, in_, eng="vector", **kw):
        if eng == "scalar":
            return self.p.emit(eng, lambda e: e.copy(out, in_), **kw)
        else:
            return self.p.emit(eng, lambda e: e.tensor_copy(out, in_), **kw)

    def tt(self, out, a, b, op, eng="vector", **kw):
        return self.p.emit(eng, lambda e: e.tensor_tensor(out, a, b, op), **kw)

    def ts(self, out, in0, s1, s2, op0, op1=None, eng="vector", **kw):
        if op1 is None:
            return self.p.emit(eng, lambda e: e.tensor_scalar(out, in0, s1, None, op0), **kw)
        else:
            return self.p.emit(eng, lambda e: e.tensor_scalar(out, in0, s1, s2, op0, op1), **kw)

    def stt(self, out, in0, scalar, in1, op0, op1, eng="vector"):
        self.p.emit(eng, lambda e: e.scalar_tensor_tensor(
            out=out, in0=in0, scalar=scalar, in1=in1, op0=op0, op1=op1))

    def act(self, out, in_, func, bias=None, scale=None, accum_out=None):
        kw = {}
        if bias is not None:
            kw["bias"] = bias
        if scale is not None:
            kw["scale"] = scale
        if accum_out is not None:
            kw["accum_out"] = accum_out
        return self.p.emit("scalar", lambda e: e.activation(out, in_, func, **kw))

    def memset(self, ap, val, eng="vector"):
        self.p.emit(eng, lambda e: e.memset(ap, val))

    def iota(self, out, pattern, base=None, channel_multiplier=None):
        self.p.emit("gpsimd", lambda e: e.iota(
            out, pattern, base=base, channel_multiplier=channel_multiplier,
            allow_small_or_imprecise_dtypes=True))

    def recip(self, out, in_):
        self.p.emit("vector", lambda e: e.reciprocal(out, in_))

    def rsum(self, out, in_):
        self.p.emit("vector", lambda e: e.reduce_sum(
            out, in_, axis=mybir.AxisListType.X))

    def finish(self):
        sems = {}
        for k in ("c", "sync", "gpsimd"):
            t = self.nc.semaphore("sem_" + k)
            sems[k] = t.__enter__()
            self._ctx.append(t)
        self.p.realize(self.nc, sems)
        for t in reversed(self._ctx):
            t.__exit__(None, None, None)
        return self.nc


def chunked_w_dma(lb, dst_sb, w_hbm_cols, nchunk=DC, groups=4, wait="chain",
                  link=True):
    view = w_hbm_cols.rearrange("(c p) n -> p c n", p=128)
    step = nchunk // groups
    tok = None
    for gidx in range(groups):
        tok = lb.din(dst_sb[:, gidx * step:(gidx + 1) * step, :],
                     view[:, gidx * step:(gidx + 1) * step, :],
                     wait=(wait if gidx == 0 else None), link=link)
    return tok


def emit_identity(lb, ident, iot):
    lb.iota(iot[:], [[1, 128]], channel_multiplier=-1)
    lb.ts(ident[:], iot[:], 0.0, None, ALU.is_equal)


def emit_ln(lb, x_hbm, xn_sb, xn_hbm, psb, xin, sqf, rows, bcast, onesf):
    ntb = TPC // 512
    ones_col = onesf[:, 0:1]
    ones_row = onesf[0:1, :]
    for c in range(DC):
        lb.din(xin[:], x_hbm[c * 128:(c + 1) * 128, :])
        lb.act(sqf[:], xin[:], AF.Square)
        for tb in range(ntb):
            lb.mm(psb[0:1, tb * 512:(tb + 1) * 512], ones_col,
                  xin[:, tb * 512:(tb + 1) * 512], start=(c == 0), stop=(c == DC - 1))
            lb.mm(psb[0:1, 1024 + tb * 512:1024 + (tb + 1) * 512], ones_col,
                  sqf[:, tb * 512:(tb + 1) * 512], start=(c == 0), stop=(c == DC - 1))
    mrow = rows[:, 0:TPC]
    vrow = rows[:, TPC:2 * TPC]
    srow = rows[:, 2 * TPC:3 * TPC]
    rrow = rows[:, 3 * TPC:4 * TPC]
    tmp = rows[:, 4 * TPC:5 * TPC]
    mrs = rows[:, 5 * TPC:6 * TPC]
    lb.ts(mrow, psb[0:1, 0:TPC], 1.0 / D, None, ALU.mult)
    lb.ts(vrow, psb[0:1, 1024:1024 + TPC], 1.0 / D, None, ALU.mult)
    lb.tt(tmp, mrow, mrow, ALU.mult)
    lb.tt(vrow, vrow, tmp, ALU.subtract)
    eps = rows[:, 6 * TPC:6 * TPC + 1]
    lb.memset(eps, 1e-5)
    lb.act(srow, vrow, AF.Sqrt, bias=eps, scale=1.0)
    lb.recip(rrow, srow)
    lb.tt(mrs, mrow, rrow, ALU.mult)
    rb = bcast[:, 0:TPC]
    mb = bcast[:, TPC:2 * TPC]
    for tb in range(ntb):
        sl = slice(tb * 512, (tb + 1) * 512)
        lb.mm(psb[:, 2048 + tb * 512:2048 + (tb + 1) * 512], ones_row, rrow[:, sl],
              start=True, stop=True)
        lb.cp(rb[:, sl], psb[:, 2048 + tb * 512:2048 + (tb + 1) * 512])
        lb.mm(psb[:, 2048 + tb * 512:2048 + (tb + 1) * 512], ones_row, mrs[:, sl],
              start=True, stop=True)
        lb.cp(mb[:, sl], psb[:, 2048 + tb * 512:2048 + (tb + 1) * 512])
    for c in range(DC):
        lb.din(xin[:], x_hbm[c * 128:(c + 1) * 128, :])
        lb.tt(sqf[:], xin[:], rb, ALU.mult)
        lb.tt(xn_sb[:, c, :], sqf[:], mb, ALU.subtract)
        if xn_hbm is not None:
            lb.dout(xn_hbm[c * 128:(c + 1) * 128, :], xn_sb[:, c, :])


def emit_proj(lb, w_hbm, bias_sb, nb, xn_sb, psb, wf32, wbf, stage, out_hbm):
    ntb = TPC // 512
    tok = chunked_w_dma(lb, wf32, w_hbm[:, nb * 128:(nb + 1) * 128])
    lb.cp(wbf[:, :, :], wf32[:, :, :])
    for tb in range(ntb):
        for c in range(DC):
            lb.mm(psb[:, tb * 512:(tb + 1) * 512], wbf[:, c, :],
                  xn_sb[:, c, tb * 512:(tb + 1) * 512],
                  start=(c == 0), stop=(c == DC - 1))
    for tb in range(ntb):
        sl = slice(tb * 512, (tb + 1) * 512)
        if bias_sb is not None:
            lb.act(stage[:, sl], psb[:, sl], AF.Identity,
                   bias=bias_sb, scale=1.0)
        else:
            lb.cp(stage[:, sl], psb[:, sl], eng="scalar")
    if out_hbm is not None:
        lb.dout(out_hbm[nb * 128:(nb + 1) * 128, :], stage[:])


def proj_loop(lb, w_hbm, xn_sb, psb, wf32, wbf, nblocks, evict_fn):
    ntb = TPC // 512
    dma_tok = chunked_w_dma(lb, wf32, w_hbm[:, 0:128])
    for nb in range(nblocks):
        cast_tok = lb.cp(wbf[:, :, :], wf32[:, :, :], extra=dma_tok)
        if nb + 1 < nblocks:
            dma_tok = chunked_w_dma(lb, wf32,
                                    w_hbm[:, (nb + 1) * 128:(nb + 2) * 128],
                                    wait=cast_tok, link=False)
        for tb in range(ntb):
            for c in range(DC):
                lb.mm(psb[:, tb * 512:(tb + 1) * 512], wbf[:, c, :],
                      xn_sb[:, c, tb * 512:(tb + 1) * 512],
                      start=(c == 0), stop=(c == DC - 1))
        evict_fn(nb)


def build_l0pre():
    lb = LB()
    xT = lb.dram_in("xT", [D, TPC], F32)
    wqT = lb.dram_in("wqT", [D, D], F32)
    wqb = lb.dram_in("wqb", [128, DC], F32)
    wkT = lb.dram_in("wkT", [D, 128], F32)
    wkb = lb.dram_in("wkb", [128, 1], F32)
    wvT = lb.dram_in("wvT", [D, 128], F32)
    wvb = lb.dram_in("wvb", [128, 1], F32)
    xnT = lb.dram_out("xnT", [D, TPC], BF16)
    qT = lb.dram_out("qT", [D, TPC], BF16)
    kT = lb.dram_out("kT", [128, TPC], BF16)
    vn = lb.dram_out("vn", [TPC, 128], BF16)

    psb = lb.ps([128, 3584], F32)
    pst = lb.ps([128, 1024], BF16)
    xin = lb.sb("xin", [128, TPC], F32)
    sqf = lb.sb("sqf", [128, TPC], F32)
    rows = lb.sb("rows", [1, 6 * TPC + 8], F32)
    bcast = lb.sb("bcast", [128, 2 * TPC], F32)
    onesf = lb.sb("onesf", [128, 128], F32)
    xnsb = lb.sb("xnsb", [128, DC, TPC], BF16)
    wf32 = lb.sb("wf32", [128, DC, 128], F32)
    wbf = lb.sb("wbf", [128, DC, 128], BF16)
    stage = lb.sb("stage", [128, TPC], BF16)
    bcol = lb.sb("bcol", [128, DC], F32)
    kvb = lb.sb("kvb", [128, 2], F32)
    ident = lb.sb("ident", [128, 128], BF16)
    iot = lb.sb("iot", [128, 128], I32)
    vns = lb.sb("vns", [128, 128], BF16)

    lb.memset(onesf[:], 1.0)
    emit_identity(lb, ident, iot)
    lb.din(bcol[:], wqb)
    lb.din(kvb[:, 0:1], wkb)
    lb.din(kvb[:, 1:2], wvb)

    emit_ln(lb, xT, xnsb, xnT, psb, xin, sqf, rows, bcast, onesf)

    def q_evict(nb):
        for tb in range(TPC // 512):
            sl = slice(tb * 512, (tb + 1) * 512)
            lb.act(stage[:, sl], psb[:, sl], AF.Identity,
                   bias=bcol[:, nb:nb + 1], scale=1.0)
        lb.dout(qT[nb * 128:(nb + 1) * 128, :], stage[:])

    proj_loop(lb, wqT, xnsb, psb, wf32, wbf, DC, q_evict)
    emit_proj(lb, wkT, kvb[:, 0:1], 0, xnsb, psb, wf32[:, :, :], wbf, stage, kT)
    emit_proj(lb, wvT, kvb[:, 1:2], 0, xnsb, psb, wf32, wbf, stage, None)
    for t8 in range(TPC // 128):
        lb.trans(pst[:, 0:128], stage[:, t8 * 128:(t8 + 1) * 128], ident[:])
        lb.cp(vns[:], pst[:, 0:128])
        lb.dout(vn[t8 * 128:(t8 + 1) * 128, :], vns[:])
    return lb


def build_attn(npairs, dh, scale):
    lb = LB()
    nit = T // 128
    dhc = dh // 128
    q8 = lb.dram_in("q8", [npairs, dh, T], BF16)
    k8 = lb.dram_in("k8", [npairs, dh, T], BF16)
    v8 = lb.dram_in("v8", [npairs, T, dh], BF16)
    dmask = lb.dram_in("dmask", [128, 128], F32)
    o8 = lb.dram_out("o8", [npairs, dh, T], BF16)
    r8 = lb.dram_out("r8", [npairs, 128, nit], F32)

    psb = lb.ps([128, 3072], F32)
    pso = lb.ps([128, 512], F32)
    pst = lb.ps([128, 1024], BF16)
    qp = lb.sb("qp", [128, dhc, T], BF16)
    kp = lb.sb("kp", [128, dhc, T], BF16)
    vp = lb.sb("vp", [128, nit, dh], BF16)
    Pslab = lb.sb("Pslab", [128, T], BF16)
    msk = lb.sb("msk", [128, 128], F32)
    rsp = lb.sb("rsp", [128, 8], F32)
    rpass = lb.sb("rpass", [128, 1], F32)
    rtile = lb.sb("rtile", [128, nit], F32)
    ptsb = lb.sb("ptsb", [128, 128], BF16)
    ostage = lb.sb("ostage", [128, dhc, 128], BF16)
    ident = lb.sb("ident", [128, 128], BF16)
    iot = lb.sb("iot", [128, 128], I32)

    emit_identity(lb, ident, iot)
    lb.din(msk[:], dmask)
    for p in range(npairs):
        lb.din(qp[:], q8[p].rearrange("(c p) n -> p c n", p=128))
        lb.din(kp[:], k8[p].rearrange("(c p) n -> p c n", p=128))
        for gr in range(4):
            st = nit // 4
            lb.din(vp[:, gr * st:(gr + 1) * st, :],
                   v8[p].rearrange("(c p) n -> p c n", p=128)[:, gr * st:(gr + 1) * st, :])
        for it in range(nit):
            span = (it + 1) * 128
            npass = (span + 3071) // 3072
            for jp in range(npass):
                j0 = jp * 3072
                w = min(3072, span - j0)
                nbk = (w + 511) // 512
                for b in range(nbk):
                    cw = min(512, w - b * 512)
                    for dc_ in range(dhc):
                        lb.mm(psb[:, b * 512:b * 512 + cw],
                              qp[:, dc_, it * 128:(it + 1) * 128],
                              kp[:, dc_, j0 + b * 512:j0 + b * 512 + cw],
                              start=(dc_ == 0), stop=(dc_ == dhc - 1))
                if jp == npass - 1:
                    off = span - 128 - j0
                    lb.tt(psb[:, off:off + 128], psb[:, off:off + 128], msk[:],
                          ALU.add)
                for b in range(nbk):
                    cw = min(512, w - b * 512)
                    lb.act(Pslab[:, j0 + b * 512:j0 + b * 512 + cw],
                           psb[:, b * 512:b * 512 + cw], AF.Exp, scale=scale,
                           accum_out=rsp[:, b:b + 1])
                if nbk > 1:
                    lb.rsum(rpass[:], rsp[:, 0:nbk])
                else:
                    lb.cp(rpass[:], rsp[:, 0:1])
                if jp == 0:
                    lb.cp(rtile[:, it:it + 1], rpass[:])
                else:
                    lb.tt(rtile[:, it:it + 1], rtile[:, it:it + 1], rpass[:],
                          ALU.add)
            for jc in range(it + 1):
                lb.trans(pst[:, 0:128], Pslab[:, jc * 128:(jc + 1) * 128],
                         ident[:])
                lb.cp(ptsb[:], pst[:, 0:128])
                for dc_ in range(dhc):
                    lb.mm(pso[:, dc_ * 128:(dc_ + 1) * 128],
                          vp[:, jc, dc_ * 128:(dc_ + 1) * 128],
                          ptsb[:],
                          start=(jc == 0 and dc_ == 0),
                          stop=(jc == it and dc_ == dhc - 1))
            lb.cp(ostage.rearrange("p c n -> p (c n)"), pso[:, 0:dhc * 128],
                  eng="scalar")
            lb.dout(o8[p].rearrange("(c p) n -> p c n", p=128)[:, :, it * 128:(it + 1) * 128],
                    ostage[:])
        lb.dout(r8[p], rtile[:])
    return lb


def build_l0post():
    lb = LB()
    oT = lb.dram_in("oT", [D, TPC], BF16)
    rc = lb.dram_in("rc", [32, TPC], F32)
    xT = lb.dram_in("xT", [D, TPC], F32)
    xnT = lb.dram_in("xnT", [D, TPC], BF16)
    woT = lb.dram_in("woT", [D, D], F32)
    wob = lb.dram_in("wob", [128, DC], F32)
    W0T = lb.dram_in("W0T", [D, D], F32)
    b0c = lb.dram_in("b0c", [128, DC], F32)
    Apk = lb.dram_in("Apk", [D, 128], F32)
    Bpk = lb.dram_in("Bpk", [128, D], F32)
    idxT = lb.dram_in("idxT", [D, TPC], U8)
    x1T = lb.dram_out("x1T", [D, TPC], F32)

    scr_r = lb.dram_scratch("scr_r", [32, TPC], F32)

    psb = lb.ps([128, 3584], F32)
    bufA = lb.sb("bufA", [128, DC, TPC], BF16)
    bufB = lb.sb("bufB", [128, DC, TPC], BF16)
    rsb = lb.sb("rsb", [32, TPC], F32)
    rrow = lb.sb("rrow", [1, TPC], F32)
    rbb = lb.sb("rbb", [128, TPC], F32)
    onesf = lb.sb("onesf", [128, 128], F32)
    osb = lb.sb("osb", [128, TPC], BF16)
    wf32 = lb.sb("wf32", [128, DC, 128], F32)
    wbf = lb.sb("wbf", [128, DC, 128], BF16)
    stage = lb.sb("stage", [128, TPC], BF16)
    bcol = lb.sb("bcol", [128, 2 * DC], F32)
    hsb = lb.sb("hsb", [128, TPC], BF16)
    Bp32 = lb.sb("Bp32", [128, 128], F32)
    Bpb = lb.sb("Bpb", [128, 128], BF16)
    idxsb = lb.sb("idxsb", [128, TPC], U8)
    mk = lb.sb("mk", [128, 512], BF16)
    accf = lb.sb("accf", [128, 512], F32)
    tmpy = lb.sb("tmpy", [128, 512], F32)
    uf = lb.sb("uf", [128, 512], F32)
    gb = lb.sb("gb", [128, 512], BF16)
    sgf = lb.sb("sgf", [128, 512], F32)
    xin = lb.sb("xin", [128, TPC], F32)
    x1s = lb.sb("x1s", [128, TPC], F32)

    lb.memset(onesf[:], 1.0)
    lb.din(bcol[:, 0:DC], wob)
    lb.din(bcol[:, DC:2 * DC], b0c)

    lb.din(rsb[:], rc)
    lb.recip(rsb[:], rsb[:])
    lb.dout(scr_r, rsb[:])
    ntb = TPC // 512
    for h in range(32):
        lb.din(rrow[:], scr_r[h:h + 1, :])
        for tb in range(ntb):
            sl = slice(tb * 512, (tb + 1) * 512)
            lb.mm(psb[:, tb * 512:(tb + 1) * 512], onesf[0:1, :], rrow[:, sl],
                  start=True, stop=True)
            lb.cp(rbb[:, sl], psb[:, tb * 512:(tb + 1) * 512])
        lb.din(osb[:], oT[h * 128:(h + 1) * 128, :])
        lb.tt(bufA[:, h, :], osb[:], rbb[:], ALU.mult)

    def wo_evict(nb):
        for tb in range(TPC // 512):
            sl = slice(tb * 512, (tb + 1) * 512)
            lb.act(stage[:, sl], psb[:, sl], AF.Identity,
                   bias=bcol[:, nb:nb + 1], scale=1.0)
        lb.cp(bufB[:, nb, :], stage[:])

    proj_loop(lb, woT, bufA, psb, wf32, wbf, DC, wo_evict)

    for c in range(DC):
        lb.din(bufA[:, c, :], xnT[c * 128:(c + 1) * 128, :])

    chunked_w_dma(lb, wf32, Apk)
    lb.cp(wbf[:, :, :], wf32[:, :, :])
    for tb in range(ntb):
        for c in range(DC):
            lb.mm(psb[:, tb * 512:(tb + 1) * 512], wbf[:, c, :],
                  bufA[:, c, tb * 512:(tb + 1) * 512],
                  start=(c == 0), stop=(c == DC - 1))
    for tb in range(ntb):
        sl = slice(tb * 512, (tb + 1) * 512)
        lb.cp(hsb[:, sl], psb[:, sl], eng="scalar")

    dma_tok = chunked_w_dma(lb, wf32, W0T[:, 0:128])
    for ob in range(DC):
        cast_tok = lb.cp(wbf[:, :, :], wf32[:, :, :], extra=dma_tok)
        if ob + 1 < DC:
            dma_tok = chunked_w_dma(lb, wf32,
                                    W0T[:, (ob + 1) * 128:(ob + 2) * 128],
                                    wait=cast_tok, link=False)
        lb.din(idxsb[:], idxT[ob * 128:(ob + 1) * 128, :])
        lb.din(Bp32[:], Bpk[:, ob * 128:(ob + 1) * 128])
        lb.cp(Bpb[:], Bp32[:])
        lb.din(xin[:], xT[ob * 128:(ob + 1) * 128, :])
        for tb in range(ntb):
            sl = slice(tb * 512, (tb + 1) * 512)
            for c in range(DC):
                lb.mm(psb[:, 0:512], wbf[:, c, :], bufA[:, c, sl],
                      start=(c == 0), stop=(c == DC - 1))
            for l in range(4):
                tp = (96, 0) if l == 3 else None
                lb.mm(psb[:, 512 * (1 + l):512 * (1 + l) + 512],
                      Bpb[32 * l:32 * l + 16, :], hsb[32 * l:32 * l + 16, sl],
                      start=True, stop=True, tile_position=tp)
            for l in range(4):
                lb.ts(mk[:], idxsb[:, sl], float(l), None, ALU.is_equal)
                if l == 0:
                    lb.tt(accf[:], psb[:, 512:1024], mk[:], ALU.mult)
                else:
                    lb.tt(tmpy[:], psb[:, 512 * (1 + l):512 * (1 + l) + 512],
                          mk[:], ALU.mult)
                    lb.tt(accf[:], accf[:], tmpy[:], ALU.add)
            lb.stt(uf[:], accf[:], 8.0, psb[:, 0:512], ALU.mult, ALU.add)
            lb.act(gb[:], uf[:], AF.Silu, bias=bcol[:, DC + ob:DC + ob + 1],
                   scale=1.0)
            lb.tt(sgf[:], bufB[:, ob, sl], gb[:], ALU.mult)
            lb.tt(x1s[:, sl], xin[:, sl], sgf[:], ALU.add)
        lb.dout(x1T[ob * 128:(ob + 1) * 128, :], x1s[:])
    return lb


def build_l1pre():
    lb = LB()
    x1T = lb.dram_in("x1T", [D, TPC], F32)
    Wq = lb.dram_in("Wq", [D, D], F32)
    Wk = lb.dram_in("Wk", [D, D], F32)
    Wv = lb.dram_in("Wv", [D, D], F32)
    cosc = lb.dram_in("cosc", [64, TPC], F32)
    sinc = lb.dram_in("sinc", [64, TPC], F32)
    xn1T = lb.dram_out("xn1T", [D, TPC], BF16)
    q1T = lb.dram_out("q1T", [D, TPC], BF16)
    k1T = lb.dram_out("k1T", [D, TPC], BF16)
    v1n = lb.dram_out("v1n", [TPC, D], BF16)

    psb = lb.ps([128, 3584], F32)
    pst = lb.ps([128, 1024], BF16)
    xin = lb.sb("xin", [128, TPC], F32)
    sqf = lb.sb("sqf", [128, TPC], F32)
    rows = lb.sb("rows", [1, 6 * TPC + 8], F32)
    bcast = lb.sb("bcast", [128, 2 * TPC], F32)
    onesf = lb.sb("onesf", [128, 128], F32)
    xnsb = lb.sb("xnsb", [128, DC, TPC], BF16)
    wf32 = lb.sb("wf32", [128, DC, 128], F32)
    wbf = lb.sb("wbf", [128, DC, 128], BF16)
    stage = lb.sb("stage", [128, TPC], BF16)
    cs = lb.sb("cs", [64, 2 * TPC], F32)
    qrb = lb.sb("qrb", [64, TPC], BF16)
    qsw = lb.sb("qsw", [64, TPC], BF16)
    ta = lb.sb("ta", [64, TPC], BF16)
    tb2 = lb.sb("tb2", [64, TPC], BF16)
    ident = lb.sb("ident", [128, 128], BF16)
    iot = lb.sb("iot", [128, 128], I32)
    vns = lb.sb("vns", [128, 128], BF16)

    lb.memset(onesf[:], 1.0)
    emit_identity(lb, ident, iot)
    lb.din(cs[:, 0:TPC], cosc)
    lb.din(cs[:, TPC:2 * TPC], sinc)

    emit_ln(lb, x1T, xnsb, xn1T, psb, xin, sqf, rows, bcast, onesf)

    ntb = TPC // 512

    def proj_out(w_hbm, dst_hbm):
        def ev(nb):
            for tb in range(ntb):
                sl = slice(tb * 512, (tb + 1) * 512)
                lb.cp(stage[:, sl], psb[:, sl], eng="scalar")
            lb.dout(dst_hbm[nb * 128:(nb + 1) * 128, :], stage[:])
        proj_loop(lb, w_hbm, xnsb, psb, wf32, wbf, DC, ev)

    def rotary(dst_hbm):
        cosr = cs[:, 0:TPC]
        sinr = cs[:, TPC:2 * TPC]
        for h in range(16):
            r0 = 256 * h
            lb.din(qrb[:], dst_hbm[r0:r0 + 64, :])
            lb.din(qsw[0:32, :], qrb[32:64, :])
            lb.din(qsw[32:64, :], qrb[0:32, :])
            lb.tt(ta[:], qrb[:], cosr, ALU.mult)
            lb.tt(tb2[:], qsw[:], sinr, ALU.mult)
            lb.tt(qrb[:], ta[:], tb2[:], ALU.add)
            lb.dout(dst_hbm[r0:r0 + 64, :], qrb[:])

    proj_out(Wq, q1T)
    rotary(q1T)
    proj_out(Wk, k1T)
    rotary(k1T)
    def v_evict(nb):
        for tb in range(ntb):
            sl = slice(tb * 512, (tb + 1) * 512)
            lb.cp(stage[:, sl], psb[:, sl], eng="scalar")
        for t8 in range(TPC // 128):
            lb.trans(pst[:, 0:128], stage[:, t8 * 128:(t8 + 1) * 128], ident[:])
            lb.cp(vns[:], pst[:, 0:128])
            lb.dout(v1n[t8 * 128:(t8 + 1) * 128, nb * 128:(nb + 1) * 128],
                    vns[:])
    proj_loop(lb, Wv, xnsb, psb, wf32, wbf, DC, v_evict)
    return lb


NI = 11008


def build_l1post():
    lb = LB()
    oT = lb.dram_in("oT", [D, TPC], BF16)
    rc = lb.dram_in("rc", [16, TPC], F32)
    x1T = lb.dram_in("x1T", [D, TPC], F32)
    xn1T = lb.dram_in("xn1T", [D, TPC], BF16)
    Wo = lb.dram_in("Wo", [D, D], F32)
    gqw = lb.dram_in("gqw", [D, NI // 8], I32)
    gsc = lb.dram_in("gsc", [32, NI], F32)
    gqz = lb.dram_in("gqz", [32, NI // 8], I32)
    uqw = lb.dram_in("uqw", [D, NI // 8], I32)
    usc = lb.dram_in("usc", [32, NI], F32)
    uqz = lb.dram_in("uqz", [32, NI // 8], I32)
    dqw = lb.dram_in("dqw", [NI, D // 8], I32)
    dsc = lb.dram_in("dsc", [86, D], F32)
    dqz = lb.dram_in("dqz", [86, D // 8], I32)
    outT = lb.dram_out("outT", [D, TPC], F32)

    scr_r = lb.dram_scratch("scr_r", [16, TPC], F32)
    zr_g = lb.dram_scratch("zr_g", [32, NI], BF16)
    sr_g = lb.dram_scratch("sr_g", [32, NI], BF16)
    zr_u = lb.dram_scratch("zr_u", [32, NI], BF16)
    sr_u = lb.dram_scratch("sr_u", [32, NI], BF16)
    zr_d = lb.dram_scratch("zr_d", [86, D], BF16)
    sr_d = lb.dram_scratch("sr_d", [86, D], BF16)
    dq_g = lb.dram_scratch("dq_g", [D, NI], BF16)
    dq_u = lb.dram_scratch("dq_u", [D, NI], BF16)
    dq_d = lb.dram_scratch("dq_d", [NI, D], BF16)
    hTs = lb.dram_scratch("hTs", [NI, TPC], BF16)

    psb = lb.ps([128, 4096], F32)
    ntb = TPC // 512

    dq_scope = []
    qwsb = lb.sb("qwsb", [128, 2, 256], I32, scope=dq_scope)
    nibsb = lb.sb("nibsb", [128, 2048], I32, scope=dq_scope)
    t1b = lb.sb("t1b", [128, 2048], BF16, scope=dq_scope)
    ssbc = lb.sb("ssbc", [128, 2048], F32, scope=dq_scope)
    wrowc = lb.sb("wrowc", [128, 2, 2048], BF16, scope=dq_scope)
    zrow = lb.sb("zrow", [1, NI], BF16, scope=dq_scope)
    srow = lb.sb("srow", [1, NI], BF16, scope=dq_scope)
    onesb = lb.sb("onesb", [1, 128], BF16, scope=dq_scope)
    lb.memset(onesb[:], 1.0)

    def unpack8(dst_i32, src_i32, chw, extra=None):
        tok = None
        for j in range(8):
            dst = dst_i32[:, CPL[j]:chw:8]
            if j == 0:
                tok = lb.ts(dst, src_i32, 15, None, ALU.bitwise_and,
                            extra=extra)
            else:
                tok = lb.ts(dst, src_i32, 4 * j, 15,
                            ALU.logical_shift_right, ALU.bitwise_and)
        return tok

    def dequant(qw, qz, sc, zr, sr, dq, G, N):
        pch = 2048
        nzch = (N + pch - 1) // pch
        for ch in range(nzch):
            c0 = ch * pch
            w = min(pch, N - c0)
            lb.din(qwsb[0:G, 0, 0:w // 8], qz[:, c0 // 8:(c0 + w) // 8])
            unpack8(nibsb[0:G, 0:w], qwsb[0:G, 0, 0:w // 8], w)
            lb.cp(t1b[0:G, 0:w], nibsb[0:G, 0:w])
            lb.dout(zr[:, c0:c0 + w], t1b[0:G, 0:w])
            lb.din(ssbc[0:G, 0:w], sc[:, c0:c0 + w])
            lb.cp(t1b[0:G, 0:w], ssbc[0:G, 0:w])
            lb.dout(sr[:, c0:c0 + w], t1b[0:G, 0:w])
        chunks = []
        c0 = 0
        while c0 < N:
            w = min(2048, N - c0)
            chunks.append((c0, w))
            c0 += w
        nch = len(chunks)
        total = G * nch
        qw_tok = {}
        dout_toks = {}
        qw_tok[0] = lb.din(qwsb[:, 0, 0:chunks[0][1] // 8],
                           qw[0:128, 0:chunks[0][1] // 8])
        for g in range(G):
            lb.din(zrow[:, 0:N], zr[g:g + 1, :])
            lb.din(srow[:, 0:N], sr[g:g + 1, :])
            for ic, (c0, w) in enumerate(chunks):
                ci = g * nch + ic
                buf = ci % 2
                for seg in range((w + 511) // 512):
                    sw = min(512, w - seg * 512)
                    lb.mm(psb[:, seg * 512:seg * 512 + sw], onesb[:],
                          zrow[0:1, c0 + seg * 512:c0 + seg * 512 + sw],
                          start=True, stop=True)
                    lb.mm(psb[:, 2048 + seg * 512:2048 + seg * 512 + sw],
                          onesb[:],
                          srow[0:1, c0 + seg * 512:c0 + seg * 512 + sw],
                          start=True, stop=True)
                ts_tok = unpack8(nibsb[:, 0:w], qwsb[:, buf, 0:w // 8], w,
                                 extra=qw_tok.pop(buf, None))
                if ci + 1 < total:
                    gg, idx = divmod(ci + 1, nch)
                    nc0, nw = chunks[idx]
                    qw_tok[(ci + 1) % 2] = lb.din(
                        qwsb[:, (ci + 1) % 2, 0:nw // 8],
                        qw[gg * 128:(gg + 1) * 128, nc0 // 8:(nc0 + nw) // 8],
                        wait=ts_tok, link=False)
                lb.tt(t1b[:, 0:w], nibsb[:, 0:w], psb[:, 0:w], ALU.subtract)
                wr_tok = lb.tt(wrowc[:, buf, 0:w], t1b[:, 0:w],
                               psb[:, 2048:2048 + w], ALU.mult,
                               extra=dout_toks.get(buf))
                last = (ci == total - 1)
                dout_toks[buf] = lb.dout(dq[g * 128:(g + 1) * 128, c0:c0 + w],
                                         wrowc[:, buf, 0:w],
                                         wait=("chain" if last else wr_tok),
                                         link=last)

    dequant(gqw, gqz, gsc, zr_g, sr_g, dq_g, 32, NI)
    dequant(uqw, uqz, usc, zr_u, sr_u, dq_u, 32, NI)
    dequant(dqw, dqz, dsc, zr_d, sr_d, dq_d, 86, D)
    lb.close_scope(dq_scope)

    bufA = lb.sb("bufA", [128, DC, TPC], BF16)
    bufB = lb.sb("bufB", [128, DC, TPC], BF16)
    rsb = lb.sb("rsb", [16, TPC], F32)
    rrow = lb.sb("rrow", [1, TPC], F32)
    rbb = lb.sb("rbb", [128, TPC], F32)
    onesf = lb.sb("onesf", [128, 128], F32)
    osb = lb.sb("osb", [128, TPC], BF16)
    scB = []
    wf32 = lb.sb("wf32", [128, DC, 128], F32, scope=scB)
    wbf = lb.sb("wbf", [128, DC, 128], BF16, scope=scB)

    lb.memset(onesf[:], 1.0)

    lb.din(rsb[:], rc)
    lb.recip(rsb[:], rsb[:])
    lb.dout(scr_r, rsb[:])
    for h in range(16):
        lb.din(rrow[:], scr_r[h:h + 1, :])
        for tb in range(ntb):
            sl = slice(tb * 512, (tb + 1) * 512)
            lb.mm(psb[:, tb * 512:(tb + 1) * 512], onesf[0:1, :], rrow[:, sl],
                  start=True, stop=True)
            lb.cp(rbb[:, sl], psb[:, tb * 512:(tb + 1) * 512])
        for c2 in range(2):
            c = 2 * h + c2
            lb.din(osb[:], oT[c * 128:(c + 1) * 128, :])
            lb.tt(bufA[:, c, :], osb[:], rbb[:], ALU.mult)

    def wo1_evict(nb):
        for tb in range(ntb):
            sl = slice(tb * 512, (tb + 1) * 512)
            lb.cp(bufB[:, nb, sl], psb[:, sl], eng="scalar")
    proj_loop(lb, Wo, bufA, psb, wf32, wbf, DC, wo1_evict)
    lb.close_scope(scB)
    scD = []
    gsl = lb.sb("gsl", [128, 2, DC, 128], BF16, scope=scD)
    usl = lb.sb("usl", [128, 2, DC, 128], BF16, scope=scD)
    gsil = lb.sb("gsil", [128, TPC], BF16, scope=scD)
    hrow = lb.sb("hrow", [128, 2, TPC], BF16, scope=scD)

    for c in range(DC):
        lb.din(bufA[:, c, :], xn1T[c * 128:(c + 1) * 128, :])
    NIB = NI // 128

    def gu_din(Ib, buf, wait):
        gview = dq_g[:, Ib * 128:(Ib + 1) * 128].rearrange("(c p) n -> p c n", p=128)
        uview = dq_u[:, Ib * 128:(Ib + 1) * 128].rearrange("(c p) n -> p c n", p=128)
        tok = None
        st = DC // 4
        for gr in range(4):
            lb.din(gsl[:, buf, gr * st:(gr + 1) * st, :],
                   gview[:, gr * st:(gr + 1) * st, :],
                   wait=(wait if gr == 0 else None), link=(wait == "chain"))
            tok = lb.din(usl[:, buf, gr * st:(gr + 1) * st, :],
                         uview[:, gr * st:(gr + 1) * st, :],
                         wait=None, link=(wait == "chain"))
        return tok

    din_tok = gu_din(0, 0, "chain")
    hout_tok = {}
    for Ib in range(NIB):
        buf = Ib % 2
        first_mm_tok = None
        for tb in range(ntb):
            for c in range(DC):
                t = lb.mm(psb[:, tb * 512:(tb + 1) * 512], gsl[:, buf, c, :],
                          bufA[:, c, tb * 512:(tb + 1) * 512],
                          start=(c == 0), stop=(c == DC - 1),
                          extra=(din_tok if (tb == 0 and c == 0) else None))
                if first_mm_tok is None:
                    first_mm_tok = t
                lb.mm(psb[:, 1024 + tb * 512:1024 + (tb + 1) * 512],
                      usl[:, buf, c, :], bufA[:, c, tb * 512:(tb + 1) * 512],
                      start=(c == 0), stop=(c == DC - 1))
        if Ib + 1 < NIB:
            din_tok = gu_din(Ib + 1, (Ib + 1) % 2, first_mm_tok)
        hb = Ib % 2
        wr_tok = None
        for tb in range(ntb):
            sl = slice(tb * 512, (tb + 1) * 512)
            lb.act(gsil[:, sl], psb[:, sl], AF.Silu)
            wr_tok = lb.tt(hrow[:, hb, sl],
                           psb[:, 1024 + tb * 512:1024 + (tb + 1) * 512],
                           gsil[:, sl], ALU.mult,
                           extra=(hout_tok.get(hb) if tb == 0 else None))
        last = (Ib == NIB - 1)
        hout_tok[hb] = lb.dout(hTs[Ib * 128:(Ib + 1) * 128, :],
                               hrow[:, hb, :],
                               wait=("chain" if last else wr_tok), link=last)

    lb.close_scope(scD)
    scE = []
    hsl = lb.sb("hsl", [128, 2, TPC], BF16, scope=scE)
    dsl = lb.sb("dsl", [128, 2, 512], BF16, scope=scE)
    gf = lb.sb("gf", [128, 512], F32, scope=scE)
    sgf = lb.sb("sgf", [128, 512], F32, scope=scE)
    xin = lb.sb("xin", [128, TPC], F32, scope=scE)
    x1s = lb.sb("x1s", [128, TPC], F32, scope=scE)
    for ds in range(8):
        def hd_din(g, buf, wait):
            lb.din(hsl[:, buf, :], hTs[g * 128:(g + 1) * 128, :],
                   wait=wait, link=(wait == "chain"))
            return lb.din(dsl[:, buf, :],
                          dq_d[g * 128:(g + 1) * 128, ds * 512:(ds + 1) * 512],
                          wait=None, link=(wait == "chain"))
        din_tok = hd_din(0, 0, "chain")
        for g in range(NIB):
            buf = g % 2
            first_tok = None
            for db in range(4):
                for tb in range(ntb):
                    t = lb.mm(psb[:, (db * 2 + tb) * 512:(db * 2 + tb + 1) * 512],
                              dsl[:, buf, db * 128:(db + 1) * 128],
                              hsl[:, buf, tb * 512:(tb + 1) * 512],
                              start=(g == 0), stop=(g == NIB - 1),
                              extra=(din_tok if (db == 0 and tb == 0) else None))
                    if first_tok is None:
                        first_tok = t
            if g + 1 < NIB:
                din_tok = hd_din(g + 1, (g + 1) % 2, first_tok)
        for db in range(4):
            dblk = ds * 4 + db
            lb.din(xin[:], x1T[dblk * 128:(dblk + 1) * 128, :])
            for tb in range(ntb):
                sl = slice(tb * 512, (tb + 1) * 512)
                lb.act(gf[:], psb[:, (db * 2 + tb) * 512:(db * 2 + tb + 1) * 512],
                       AF.Silu)
                lb.tt(sgf[:], bufB[:, dblk, sl], gf[:], ALU.mult)
                lb.tt(x1s[:, sl], xin[:, sl], sgf[:], ALU.add)
            lb.dout(outT[dblk * 128:(dblk + 1) * 128, :], x1s[:])
    lb.close_scope(scE)
    return lb


def _run(lb, in_maps):
    nc = lb.finish()
    res = run_bass_kernel_spmd(nc, in_maps, core_ids=list(range(NCORES)))
    return res.results


def _cont(a):
    return np.ascontiguousarray(a)


def _threefry2x32(k1, k2, x0, x1):
    U = np.uint32

    def rotl(x, d):
        return (x << U(d)) | (x >> U(32 - d))

    ks0, ks1 = U(k1), U(k2)
    ks2 = ks0 ^ ks1 ^ U(0x1BD11BDA)
    x = [x0.astype(U) + ks0, x1.astype(U) + ks1]

    def rounds(x, rots):
        for r in rots:
            x[0] = x[0] + x[1]
            x[1] = rotl(x[1], r) ^ x[0]
        return x

    r0 = [13, 15, 26, 6]
    r1 = [17, 29, 16, 24]
    x = rounds(x, r0)
    x[0] += ks1
    x[1] += ks2 + U(1)
    x = rounds(x, r1)
    x[0] += ks2
    x[1] += ks0 + U(2)
    x = rounds(x, r0)
    x[0] += ks0
    x[1] += ks1 + U(3)
    x = rounds(x, r1)
    x[0] += ks1
    x[1] += ks2 + U(4)
    x = rounds(x, r0)
    x[0] += ks2
    x[1] += ks0 + U(5)
    return x[0], x[1]


def _mask_idx():
    n = B * T * D
    counts = np.arange(n, dtype=np.uint64)
    c_hi = (counts >> np.uint64(32)).astype(np.uint32)
    c_lo = (counts & np.uint64(0xFFFFFFFF)).astype(np.uint32)
    b1, b2 = _threefry2x32(np.uint32(0), np.uint32(42), c_hi, c_lo)
    bits = b1 ^ b2
    fb = (bits >> np.uint32(9)) | np.uint32(0x3F800000)
    u = fb.view(np.float32) - np.float32(1.0)
    idx = ((u >= 0.25).astype(np.uint8) + (u >= 0.5).astype(np.uint8)
           + (u >= 0.75).astype(np.uint8))
    return idx.reshape(B, T, D)


def _rot_tables():
    inv_freq = (1.0 / 10000.0 ** (np.arange(0, 64, 2, dtype=np.float32) / 64.0)
                ).astype(np.float32)
    t = np.arange(T, dtype=np.float32)
    sinu = np.einsum("i,j->ij", t, inv_freq).astype(np.float32)
    return np.sin(sinu).astype(np.float32), np.cos(sinu).astype(np.float32)


def _l1_perm():
    perm = np.zeros(D, dtype=np.int64)
    for h in range(16):
        base = 256 * h
        newo = ([base + 2 * i for i in range(32)]
                + [base + 2 * i + 1 for i in range(32)]
                + [base + i for i in range(64, 256)])
        perm[base:base + 256] = newo
    return perm


def kernel(**kw):
    return _forward(kw, dbg=None)


def _forward(kw, dbg=None):
    (L0_attn_x, L0_attn_wq_w, L0_attn_wq_b, L0_attn_wk_w, L0_attn_wk_b,
     L0_attn_wv_w, L0_attn_wv_b, L0_attn_wo_w, L0_attn_wo_b,
     L0_ffn_W0, L0_ffn_b0, L0_ffn_A, L0_ffn_B,
     L1_attn_Wq, L1_attn_Wk, L1_attn_Wv, L1_attn_Wo,
     L1_ffn_gate_qweight, L1_ffn_gate_scales, L1_ffn_gate_qzeros,
     L1_ffn_up_qweight, L1_ffn_up_scales, L1_ffn_up_qzeros,
     L1_ffn_down_qweight, L1_ffn_down_scales, L1_ffn_down_qzeros) = (
        kw[n] for n in (
            "L0_attn_x", "L0_attn_wq_w", "L0_attn_wq_b", "L0_attn_wk_w",
            "L0_attn_wk_b", "L0_attn_wv_w", "L0_attn_wv_b", "L0_attn_wo_w",
            "L0_attn_wo_b", "L0_ffn_W0", "L0_ffn_b0", "L0_ffn_A", "L0_ffn_B",
            "L1_attn_Wq", "L1_attn_Wk", "L1_attn_Wv", "L1_attn_Wo",
            "L1_ffn_gate_qweight", "L1_ffn_gate_scales", "L1_ffn_gate_qzeros",
            "L1_ffn_up_qweight", "L1_ffn_up_scales", "L1_ffn_up_qzeros",
            "L1_ffn_down_qweight", "L1_ffn_down_scales", "L1_ffn_down_qzeros"))
    f32 = np.float32

    x = np.asarray(L0_attn_x, dtype=f32)
    xT = _cont(x.reshape(TOK, D).T)

    wqT = _cont(np.asarray(L0_attn_wq_w, f32).T)
    wqb = _cont(np.asarray(L0_attn_wq_b, f32).reshape(DC, 128).T)
    wkT = _cont(np.asarray(L0_attn_wk_w, f32).T)
    wkb = _cont(np.asarray(L0_attn_wk_b, f32)[:, None])
    wvT = _cont(np.asarray(L0_attn_wv_w, f32).T)
    wvb = _cont(np.asarray(L0_attn_wv_b, f32)[:, None])

    lb = build_l0pre()
    in_maps = []
    for c in range(NCORES):
        in_maps.append({
            "xT": _cont(xT[:, c * TPC:(c + 1) * TPC]),
            "wqT": wqT, "wqb": wqb, "wkT": wkT, "wkb": wkb,
            "wvT": wvT, "wvb": wvb,
        })
    r1 = _run(lb, in_maps)
    xnT_g = np.concatenate([r1[c]["xnT"] for c in range(NCORES)], axis=1)
    qT_g = np.concatenate([r1[c]["qT"] for c in range(NCORES)], axis=1)
    kT_g = np.concatenate([r1[c]["kT"] for c in range(NCORES)], axis=1)
    vn_g = np.concatenate([r1[c]["vn"] for c in range(NCORES)], axis=0)
    if dbg is not None:
        dbg.update(xnT=xnT_g, qT=qT_g, kT=kT_g, vn=vn_g)
        if dbg.get("stop") == 1:
            return None

    dmask = np.triu(np.full((128, 128), -1e30, dtype=f32), k=1)
    lb = build_attn(8, 128, 1.0 / math.sqrt(128.0))
    in_maps = []
    for c in range(NCORES):
        q8 = np.zeros((8, 128, T), dtype=bfdt)
        k8 = np.zeros((8, 128, T), dtype=bfdt)
        v8 = np.zeros((8, T, 128), dtype=bfdt)
        for p in range(8):
            gpair = c * 8 + p
            b, h = gpair // 32, gpair % 32
            q8[p] = qT_g[h * 128:(h + 1) * 128, b * T:(b + 1) * T]
            k8[p] = kT_g[:, b * T:(b + 1) * T]
            v8[p] = vn_g[b * T:(b + 1) * T, :]
        in_maps.append({"q8": q8, "k8": k8, "v8": v8, "dmask": dmask})
    r2 = _run(lb, in_maps)

    oT_g = np.zeros((D, TOK), dtype=bfdt)
    r_g = np.zeros((32, TOK), dtype=f32)
    for c in range(NCORES):
        for p in range(8):
            gpair = c * 8 + p
            b, h = gpair // 32, gpair % 32
            oT_g[h * 128:(h + 1) * 128, b * T:(b + 1) * T] = r2[c]["o8"][p]
            r_g[h, b * T:(b + 1) * T] = _cont(r2[c]["r8"][p].T).reshape(T)
    if dbg is not None:
        dbg.update(oT=oT_g, r=r_g)
        if dbg.get("stop") == 2:
            return None

    woT = _cont(np.asarray(L0_attn_wo_w, f32).T)
    wob = _cont(np.asarray(L0_attn_wo_b, f32).reshape(DC, 128).T)
    W0T = _cont(np.asarray(L0_ffn_W0, f32).T)
    b0c = _cont(np.asarray(L0_ffn_b0, f32).reshape(DC, 128).T)
    A = np.asarray(L0_ffn_A, f32)
    Bm = np.asarray(L0_ffn_B, f32)
    Apk = np.zeros((D, 128), dtype=f32)
    Bpk = np.zeros((128, D), dtype=f32)
    for l in range(4):
        Apk[:, 32 * l:32 * l + 16] = A[l].T
        Bpk[32 * l:32 * l + 16, :] = Bm[l].T
    idx = _mask_idx()
    idxT_g = _cont(idx.reshape(TOK, D).T)

    lb = build_l0post()
    in_maps = []
    for c in range(NCORES):
        sl = slice(c * TPC, (c + 1) * TPC)
        in_maps.append({
            "oT": _cont(oT_g[:, sl]), "rc": _cont(r_g[:, sl]),
            "xT": _cont(xT[:, sl]), "xnT": _cont(xnT_g[:, sl]),
            "woT": woT, "wob": wob, "W0T": W0T, "b0c": b0c,
            "Apk": Apk, "Bpk": Bpk, "idxT": _cont(idxT_g[:, sl]),
        })
    r3 = _run(lb, in_maps)
    x1T_g = np.concatenate([r3[c]["x1T"] for c in range(NCORES)], axis=1)
    if dbg is not None:
        dbg.update(x1T=x1T_g)
        if dbg.get("stop") == 3:
            return None

    perm = _l1_perm()
    Wq = _cont(np.asarray(L1_attn_Wq, f32)[:, perm])
    Wk = _cont(np.asarray(L1_attn_Wk, f32)[:, perm])
    Wv = _cont(np.asarray(L1_attn_Wv, f32))
    sint, cost = _rot_tables()
    cosc_g = _cont(np.tile(np.concatenate([cost.T, cost.T], axis=0), (1, B)))
    sinc_g = _cont(np.tile(np.concatenate([sint.T, sint.T], axis=0), (1, B)))

    lb = build_l1pre()
    in_maps = []
    for c in range(NCORES):
        sl = slice(c * TPC, (c + 1) * TPC)
        in_maps.append({
            "x1T": _cont(x1T_g[:, sl]), "Wq": Wq, "Wk": Wk, "Wv": Wv,
            "cosc": _cont(cosc_g[:, sl]), "sinc": _cont(sinc_g[:, sl]),
        })
    r4 = _run(lb, in_maps)
    xn1T_g = np.concatenate([r4[c]["xn1T"] for c in range(NCORES)], axis=1)
    q1T_g = np.concatenate([r4[c]["q1T"] for c in range(NCORES)], axis=1)
    k1T_g = np.concatenate([r4[c]["k1T"] for c in range(NCORES)], axis=1)
    v1n_g = np.concatenate([r4[c]["v1n"] for c in range(NCORES)], axis=0)
    if dbg is not None:
        dbg.update(xn1T=xn1T_g, q1T=q1T_g, k1T=k1T_g, v1n=v1n_g)
        if dbg.get("stop") == 4:
            return None

    lb = build_attn(4, 256, 1.0 / 16.0)
    in_maps = []
    for c in range(NCORES):
        q4 = np.zeros((4, 256, T), dtype=bfdt)
        k4 = np.zeros((4, 256, T), dtype=bfdt)
        v4 = np.zeros((4, T, 256), dtype=bfdt)
        for p in range(4):
            gpair = c * 4 + p
            b, h = gpair // 16, gpair % 16
            q4[p] = q1T_g[h * 256:(h + 1) * 256, b * T:(b + 1) * T]
            k4[p] = k1T_g[h * 256:(h + 1) * 256, b * T:(b + 1) * T]
            v4[p] = v1n_g[b * T:(b + 1) * T, h * 256:(h + 1) * 256]
        in_maps.append({"q8": q4, "k8": k4, "v8": v4, "dmask": dmask})
    r5 = _run(lb, in_maps)

    o1T_g = np.zeros((D, TOK), dtype=bfdt)
    r1_g = np.zeros((16, TOK), dtype=f32)
    for c in range(NCORES):
        for p in range(4):
            gpair = c * 4 + p
            b, h = gpair // 16, gpair % 16
            o1T_g[h * 256:(h + 1) * 256, b * T:(b + 1) * T] = r5[c]["o8"][p]
            r1_g[h, b * T:(b + 1) * T] = _cont(r5[c]["r8"][p].T).reshape(T)
    if dbg is not None:
        dbg.update(o1T=o1T_g, r1=r1_g)
        if dbg.get("stop") == 5:
            return None

    Wo1 = _cont(np.asarray(L1_attn_Wo, f32))
    lb = build_l1post()
    in_maps = []
    for c in range(NCORES):
        sl = slice(c * TPC, (c + 1) * TPC)
        in_maps.append({
            "oT": _cont(o1T_g[:, sl]), "rc": _cont(r1_g[:, sl]),
            "x1T": _cont(x1T_g[:, sl]), "xn1T": _cont(xn1T_g[:, sl]),
            "Wo": Wo1,
            "gqw": np.asarray(L1_ffn_gate_qweight, np.int32),
            "gsc": np.asarray(L1_ffn_gate_scales, f32),
            "gqz": np.asarray(L1_ffn_gate_qzeros, np.int32),
            "uqw": np.asarray(L1_ffn_up_qweight, np.int32),
            "usc": np.asarray(L1_ffn_up_scales, f32),
            "uqz": np.asarray(L1_ffn_up_qzeros, np.int32),
            "dqw": np.asarray(L1_ffn_down_qweight, np.int32),
            "dsc": np.asarray(L1_ffn_down_scales, f32),
            "dqz": np.asarray(L1_ffn_down_qzeros, np.int32),
        })
    r6 = _run(lb, in_maps)
    outT = np.concatenate([r6[c]["outT"] for c in range(NCORES)], axis=1)
    return _cont(outT.T.astype(np.float32)).reshape(B, T, D)


# version 05 (final)
# speedup vs baseline: 1.7704x; 1.7087x over previous
import math
import numpy as np
import ml_dtypes
import concourse.bass as bass
import concourse.mybir as mybir
from concourse.bass_utils import run_bass_kernel_spmd

F32 = mybir.dt.float32
BF16 = mybir.dt.bfloat16
I32 = mybir.dt.int32
U8 = mybir.dt.uint8
AF = mybir.ActivationFunctionType
ALU = mybir.AluOpType
bfdt = ml_dtypes.bfloat16

D = 4096
B = 2
T = 4096
TOK = B * T
NCORES = 8
TPC = TOK // NCORES
DC = D // 128
CPL = [0, 2, 4, 6, 1, 3, 5, 7]

ENGS = ("tensor", "vector", "scalar", "sync", "gpsimd")


class Prog:

    def __init__(self):
        self.items = {e: [] for e in ENGS}
        self.counts = {"c": 0, "sync": 0, "gpsimd": 0}
        self.last = None
        self.last_kind = None

    def emit(self, eng, fn, dma=False, wait="chain", link=True, extra=None,
             kind=None):
        waits = []
        if wait == "chain":
            skip = (kind is not None and kind == self.last_kind
                    and kind.startswith("mm"))
            if self.last is not None and not skip:
                waits.append(self.last)
        elif wait is not None:
            waits.append(wait)
        if extra is not None:
            waits.append(extra)
        if dma:
            assert eng in ("sync", "gpsimd")
            self.counts[eng] += 16
            inc = (eng, 16)
            tail = (eng, self.counts[eng])
        else:
            self.counts["c"] += 1
            inc = ("c", 1)
            tail = ("c", self.counts["c"])
        if link:
            self.last = tail
            self.last_kind = kind
        self.items[eng].append((waits, fn, inc))
        return tail

    def realize(self, nc, sems):
        with nc.Block() as blk:
            for ename in ENGS:
                if not self.items[ename]:
                    continue

                def make_body(items):
                    def body(eng):
                        for waits, fn, inc in items:
                            for w in waits:
                                eng.wait_ge(sems[w[0]], w[1])
                            inst = fn(eng)
                            inst.then_inc(sems[inc[0]], inc[1])
                    return body

                getattr(blk, ename)(make_body(self.items[ename]))
            if self.last is not None:
                last = self.last

                def fence(eng):
                    eng.wait_ge(sems[last[0]], last[1])

                blk.gpsimd(fence)


class LB:

    def __init__(self):
        self.nc = bass.Bass("TRN2", target_bir_lowering=False, debug=False)
        self.p = Prog()
        self._ctx = []
        self.inputs = []
        self.outputs = []

    def dram_in(self, name, shape, dtype):
        self.inputs.append(name)
        return self.nc.dram_tensor(name, shape, dtype, kind="ExternalInput").ap()

    def dram_out(self, name, shape, dtype):
        self.outputs.append(name)
        return self.nc.dram_tensor(name, shape, dtype, kind="ExternalOutput").ap()

    def dram_scratch(self, name, shape, dtype):
        return self.nc.dram_tensor(name, shape, dtype, kind="Internal").ap()

    def sb(self, name, shape, dtype, scope=None):
        t = self.nc.sbuf_tensor(name, shape, dtype)
        h = t.__enter__()
        (self._ctx if scope is None else scope).append(t)
        return h

    def close_scope(self, scope):
        for t in reversed(scope):
            t.__exit__(None, None, None)

    def ps(self, shape, dtype):
        t = self.nc.psum_tensor(shape, dtype)
        h = t.__enter__()
        self._ctx.append(t)
        return h

    def din(self, dst, src, **kw):
        return self.p.emit("sync", lambda e: e.dma_start(out=dst, in_=src),
                           dma=True, **kw)

    def dout(self, dst, src, **kw):
        return self.p.emit("gpsimd", lambda e: e.dma_start(out=dst, in_=src),
                           dma=True, **kw)

    def mm(self, out, lhsT, rhs, start, stop, tile_position=None,
           kind="mm", **kw):
        return self.p.emit("tensor", lambda e: e.matmul(
            out, lhsT, rhs, start=start, stop=stop,
            skip_group_check=True, tile_position=tile_position),
            kind=kind, **kw)

    def trans(self, out, in_, ident):
        self.p.emit("tensor", lambda e: e.transpose(out, in_, ident))

    def cp(self, out, in_, eng="vector", **kw):
        if eng == "scalar":
            return self.p.emit(eng, lambda e: e.copy(out, in_), **kw)
        else:
            return self.p.emit(eng, lambda e: e.tensor_copy(out, in_), **kw)

    def tt(self, out, a, b, op, eng="vector", **kw):
        return self.p.emit(eng, lambda e: e.tensor_tensor(out, a, b, op), **kw)

    def ts(self, out, in0, s1, s2, op0, op1=None, eng="vector", **kw):
        if op1 is None:
            return self.p.emit(eng, lambda e: e.tensor_scalar(out, in0, s1, None, op0), **kw)
        else:
            return self.p.emit(eng, lambda e: e.tensor_scalar(out, in0, s1, s2, op0, op1), **kw)

    def stt(self, out, in0, scalar, in1, op0, op1, eng="vector"):
        self.p.emit(eng, lambda e: e.scalar_tensor_tensor(
            out=out, in0=in0, scalar=scalar, in1=in1, op0=op0, op1=op1))

    def act(self, out, in_, func, bias=None, scale=None, accum_out=None):
        kw = {}
        if bias is not None:
            kw["bias"] = bias
        if scale is not None:
            kw["scale"] = scale
        if accum_out is not None:
            kw["accum_out"] = accum_out
        return self.p.emit("scalar", lambda e: e.activation(out, in_, func, **kw))

    def memset(self, ap, val, eng="vector"):
        self.p.emit(eng, lambda e: e.memset(ap, val))

    def iota(self, out, pattern, base=None, channel_multiplier=None):
        self.p.emit("gpsimd", lambda e: e.iota(
            out, pattern, base=base, channel_multiplier=channel_multiplier,
            allow_small_or_imprecise_dtypes=True))

    def recip(self, out, in_):
        self.p.emit("vector", lambda e: e.reciprocal(out, in_))

    def rsum(self, out, in_):
        self.p.emit("vector", lambda e: e.reduce_sum(
            out, in_, axis=mybir.AxisListType.X))

    def finish(self):
        sems = {}
        for k in ("c", "sync", "gpsimd"):
            t = self.nc.semaphore("sem_" + k)
            sems[k] = t.__enter__()
            self._ctx.append(t)
        self.p.realize(self.nc, sems)
        for t in reversed(self._ctx):
            t.__exit__(None, None, None)
        return self.nc


def chunked_w_dma(lb, dst_sb, w_hbm_cols, nchunk=DC, groups=4, wait="chain",
                  link=True):
    view = w_hbm_cols.rearrange("(c p) n -> p c n", p=128)
    step = nchunk // groups
    tok = None
    for gidx in range(groups):
        tok = lb.din(dst_sb[:, gidx * step:(gidx + 1) * step, :],
                     view[:, gidx * step:(gidx + 1) * step, :],
                     wait=(wait if gidx == 0 else None), link=link)
    return tok


def emit_identity(lb, ident, iot):
    lb.iota(iot[:], [[1, 128]], channel_multiplier=-1)
    lb.ts(ident[:], iot[:], 0.0, None, ALU.is_equal)


def emit_ln(lb, x_hbm, xn_sb, xn_hbm, psb, xin, sqf, rows, bcast, onesf):
    ntb = TPC // 512
    ones_col = onesf[:, 0:1]
    ones_row = onesf[0:1, :]
    for c in range(DC):
        lb.din(xin[:], x_hbm[c * 128:(c + 1) * 128, :])
        lb.act(sqf[:], xin[:], AF.Square)
        for tb in range(ntb):
            lb.mm(psb[0:1, tb * 512:(tb + 1) * 512], ones_col,
                  xin[:, tb * 512:(tb + 1) * 512], start=(c == 0), stop=(c == DC - 1),
                  kind="mm_s")
            lb.mm(psb[0:1, 1024 + tb * 512:1024 + (tb + 1) * 512], ones_col,
                  sqf[:, tb * 512:(tb + 1) * 512], start=(c == 0), stop=(c == DC - 1),
                  kind="mm_s")
    mrow = rows[:, 0:TPC]
    vrow = rows[:, TPC:2 * TPC]
    srow = rows[:, 2 * TPC:3 * TPC]
    rrow = rows[:, 3 * TPC:4 * TPC]
    tmp = rows[:, 4 * TPC:5 * TPC]
    mrs = rows[:, 5 * TPC:6 * TPC]
    lb.ts(mrow, psb[0:1, 0:TPC], 1.0 / D, None, ALU.mult)
    lb.ts(vrow, psb[0:1, 1024:1024 + TPC], 1.0 / D, None, ALU.mult)
    lb.tt(tmp, mrow, mrow, ALU.mult)
    lb.tt(vrow, vrow, tmp, ALU.subtract)
    eps = rows[:, 6 * TPC:6 * TPC + 1]
    lb.memset(eps, 1e-5)
    lb.act(srow, vrow, AF.Sqrt, bias=eps, scale=1.0)
    lb.recip(rrow, srow)
    lb.tt(mrs, mrow, rrow, ALU.mult)
    rb = bcast[:, 0:TPC]
    mb = bcast[:, TPC:2 * TPC]
    for tb in range(ntb):
        sl = slice(tb * 512, (tb + 1) * 512)
        lb.mm(psb[:, 2048 + tb * 512:2048 + (tb + 1) * 512], ones_row, rrow[:, sl],
              start=True, stop=True, kind="mm_b")
        lb.cp(rb[:, sl], psb[:, 2048 + tb * 512:2048 + (tb + 1) * 512])
        lb.mm(psb[:, 2048 + tb * 512:2048 + (tb + 1) * 512], ones_row, mrs[:, sl],
              start=True, stop=True, kind="mm_b")
        lb.cp(mb[:, sl], psb[:, 2048 + tb * 512:2048 + (tb + 1) * 512])
    for c in range(DC):
        lb.din(xin[:], x_hbm[c * 128:(c + 1) * 128, :])
        lb.tt(sqf[:], xin[:], rb, ALU.mult)
        lb.tt(xn_sb[:, c, :], sqf[:], mb, ALU.subtract)
        if xn_hbm is not None:
            lb.dout(xn_hbm[c * 128:(c + 1) * 128, :], xn_sb[:, c, :])


def emit_proj(lb, w_hbm, bias_sb, nb, xn_sb, psb, wf32, wbf, stage, out_hbm):
    ntb = TPC // 512
    tok = chunked_w_dma(lb, wf32, w_hbm[:, nb * 128:(nb + 1) * 128])
    lb.cp(wbf[:, :, :], wf32[:, :, :])
    for tb in range(ntb):
        for c in range(DC):
            lb.mm(psb[:, tb * 512:(tb + 1) * 512], wbf[:, c, :],
                  xn_sb[:, c, tb * 512:(tb + 1) * 512],
                  start=(c == 0), stop=(c == DC - 1))
    for tb in range(ntb):
        sl = slice(tb * 512, (tb + 1) * 512)
        if bias_sb is not None:
            lb.act(stage[:, sl], psb[:, sl], AF.Identity,
                   bias=bias_sb, scale=1.0)
        else:
            lb.cp(stage[:, sl], psb[:, sl], eng="scalar")
    if out_hbm is not None:
        lb.dout(out_hbm[nb * 128:(nb + 1) * 128, :], stage[:])


def proj_loop(lb, w_hbm, xn_sb, psb, wf32, wbf, nblocks, evict_fn):
    ntb = TPC // 512
    dma_tok = chunked_w_dma(lb, wf32, w_hbm[:, 0:128])
    for nb in range(nblocks):
        cast_tok = lb.cp(wbf[:, :, :], wf32[:, :, :], extra=dma_tok)
        if nb + 1 < nblocks:
            dma_tok = chunked_w_dma(lb, wf32,
                                    w_hbm[:, (nb + 1) * 128:(nb + 2) * 128],
                                    wait=cast_tok, link=False)
        for tb in range(ntb):
            for c in range(DC):
                lb.mm(psb[:, tb * 512:(tb + 1) * 512], wbf[:, c, :],
                      xn_sb[:, c, tb * 512:(tb + 1) * 512],
                      start=(c == 0), stop=(c == DC - 1))
        evict_fn(nb)


def build_l0pre():
    lb = LB()
    xT = lb.dram_in("xT", [D, TPC], F32)
    wqT = lb.dram_in("wqT", [D, D], F32)
    wqb = lb.dram_in("wqb", [128, DC], F32)
    wkT = lb.dram_in("wkT", [D, 128], F32)
    wkb = lb.dram_in("wkb", [128, 1], F32)
    wvT = lb.dram_in("wvT", [D, 128], F32)
    wvb = lb.dram_in("wvb", [128, 1], F32)
    xnT = lb.dram_out("xnT", [D, TPC], BF16)
    qT = lb.dram_out("qT", [D, TPC], BF16)
    kT = lb.dram_out("kT", [128, TPC], BF16)
    vn = lb.dram_out("vn", [TPC, 128], BF16)

    psb = lb.ps([128, 3584], F32)
    pst = lb.ps([128, 1024], BF16)
    xin = lb.sb("xin", [128, TPC], F32)
    sqf = lb.sb("sqf", [128, TPC], F32)
    rows = lb.sb("rows", [1, 6 * TPC + 8], F32)
    bcast = lb.sb("bcast", [128, 2 * TPC], F32)
    onesf = lb.sb("onesf", [128, 128], F32)
    xnsb = lb.sb("xnsb", [128, DC, TPC], BF16)
    wf32 = lb.sb("wf32", [128, DC, 128], F32)
    wbf = lb.sb("wbf", [128, DC, 128], BF16)
    stage = lb.sb("stage", [128, TPC], BF16)
    bcol = lb.sb("bcol", [128, DC], F32)
    kvb = lb.sb("kvb", [128, 2], F32)
    ident = lb.sb("ident", [128, 128], BF16)
    iot = lb.sb("iot", [128, 128], I32)
    vns = lb.sb("vns", [128, 128], BF16)

    lb.memset(onesf[:], 1.0)
    emit_identity(lb, ident, iot)
    lb.din(bcol[:], wqb)
    lb.din(kvb[:, 0:1], wkb)
    lb.din(kvb[:, 1:2], wvb)

    emit_ln(lb, xT, xnsb, xnT, psb, xin, sqf, rows, bcast, onesf)

    def q_evict(nb):
        for tb in range(TPC // 512):
            sl = slice(tb * 512, (tb + 1) * 512)
            lb.act(stage[:, sl], psb[:, sl], AF.Identity,
                   bias=bcol[:, nb:nb + 1], scale=1.0)
        lb.dout(qT[nb * 128:(nb + 1) * 128, :], stage[:])

    proj_loop(lb, wqT, xnsb, psb, wf32, wbf, DC, q_evict)
    emit_proj(lb, wkT, kvb[:, 0:1], 0, xnsb, psb, wf32[:, :, :], wbf, stage, kT)
    emit_proj(lb, wvT, kvb[:, 1:2], 0, xnsb, psb, wf32, wbf, stage, None)
    for t8 in range(TPC // 128):
        lb.trans(pst[:, 0:128], stage[:, t8 * 128:(t8 + 1) * 128], ident[:])
        lb.cp(vns[:], pst[:, 0:128])
        lb.dout(vn[t8 * 128:(t8 + 1) * 128, :], vns[:])
    return lb


def build_attn(npairs, dh, scale):
    lb = LB()
    nit = T // 128
    dhc = dh // 128
    q8 = lb.dram_in("q8", [npairs, dh, T], BF16)
    k8 = lb.dram_in("k8", [npairs, dh, T], BF16)
    v8 = lb.dram_in("v8", [npairs, T, dh], BF16)
    dmask = lb.dram_in("dmask", [128, 128], F32)
    o8 = lb.dram_out("o8", [npairs, dh, T], BF16)
    r8 = lb.dram_out("r8", [npairs, 128, nit], F32)

    psb = lb.ps([128, 3072], F32)
    pso = lb.ps([128, 512], F32)
    pst = lb.ps([128, 1024], BF16)
    qp = lb.sb("qp", [128, dhc, T], BF16)
    kp = lb.sb("kp", [128, dhc, T], BF16)
    vp = lb.sb("vp", [128, nit, dh], BF16)
    Pslab = lb.sb("Pslab", [128, T], BF16)
    msk = lb.sb("msk", [128, 128], F32)
    rsp = lb.sb("rsp", [128, 8], F32)
    rpass = lb.sb("rpass", [128, 1], F32)
    rtile = lb.sb("rtile", [128, nit], F32)
    ptsb = lb.sb("ptsb", [128, 128], BF16)
    ostage = lb.sb("ostage", [128, dhc, 128], BF16)
    ident = lb.sb("ident", [128, 128], BF16)
    iot = lb.sb("iot", [128, 128], I32)

    emit_identity(lb, ident, iot)
    lb.din(msk[:], dmask)
    for p in range(npairs):
        lb.din(qp[:], q8[p].rearrange("(c p) n -> p c n", p=128))
        lb.din(kp[:], k8[p].rearrange("(c p) n -> p c n", p=128))
        for gr in range(4):
            st = nit // 4
            lb.din(vp[:, gr * st:(gr + 1) * st, :],
                   v8[p].rearrange("(c p) n -> p c n", p=128)[:, gr * st:(gr + 1) * st, :])
        for it in range(nit):
            span = (it + 1) * 128
            npass = (span + 3071) // 3072
            for jp in range(npass):
                j0 = jp * 3072
                w = min(3072, span - j0)
                nbk = (w + 511) // 512
                for b in range(nbk):
                    cw = min(512, w - b * 512)
                    for dc_ in range(dhc):
                        lb.mm(psb[:, b * 512:b * 512 + cw],
                              qp[:, dc_, it * 128:(it + 1) * 128],
                              kp[:, dc_, j0 + b * 512:j0 + b * 512 + cw],
                              start=(dc_ == 0), stop=(dc_ == dhc - 1))
                if jp == npass - 1:
                    off = span - 128 - j0
                    lb.tt(psb[:, off:off + 128], psb[:, off:off + 128], msk[:],
                          ALU.add)
                for b in range(nbk):
                    cw = min(512, w - b * 512)
                    lb.act(Pslab[:, j0 + b * 512:j0 + b * 512 + cw],
                           psb[:, b * 512:b * 512 + cw], AF.Exp, scale=scale,
                           accum_out=rsp[:, b:b + 1])
                if nbk > 1:
                    lb.rsum(rpass[:], rsp[:, 0:nbk])
                else:
                    lb.cp(rpass[:], rsp[:, 0:1])
                if jp == 0:
                    lb.cp(rtile[:, it:it + 1], rpass[:])
                else:
                    lb.tt(rtile[:, it:it + 1], rtile[:, it:it + 1], rpass[:],
                          ALU.add)
            for jc in range(it + 1):
                lb.trans(pst[:, 0:128], Pslab[:, jc * 128:(jc + 1) * 128],
                         ident[:])
                lb.cp(ptsb[:], pst[:, 0:128])
                for dc_ in range(dhc):
                    lb.mm(pso[:, dc_ * 128:(dc_ + 1) * 128],
                          vp[:, jc, dc_ * 128:(dc_ + 1) * 128],
                          ptsb[:],
                          start=(jc == 0 and dc_ == 0),
                          stop=(jc == it and dc_ == dhc - 1))
            lb.cp(ostage.rearrange("p c n -> p (c n)"), pso[:, 0:dhc * 128],
                  eng="scalar")
            lb.dout(o8[p].rearrange("(c p) n -> p c n", p=128)[:, :, it * 128:(it + 1) * 128],
                    ostage[:])
        lb.dout(r8[p], rtile[:])
    return lb


def build_l0post():
    lb = LB()
    oT = lb.dram_in("oT", [D, TPC], BF16)
    rc = lb.dram_in("rc", [32, TPC], F32)
    xT = lb.dram_in("xT", [D, TPC], F32)
    xnT = lb.dram_in("xnT", [D, TPC], BF16)
    woT = lb.dram_in("woT", [D, D], F32)
    wob = lb.dram_in("wob", [128, DC], F32)
    W0T = lb.dram_in("W0T", [D, D], F32)
    b0c = lb.dram_in("b0c", [128, DC], F32)
    Apk = lb.dram_in("Apk", [D, 128], F32)
    Bpk = lb.dram_in("Bpk", [128, D], F32)
    idxT = lb.dram_in("idxT", [D, TPC], U8)
    x1T = lb.dram_out("x1T", [D, TPC], F32)

    scr_r = lb.dram_scratch("scr_r", [32, TPC], F32)

    psb = lb.ps([128, 3584], F32)
    bufA = lb.sb("bufA", [128, DC, TPC], BF16)
    bufB = lb.sb("bufB", [128, DC, TPC], BF16)
    rsb = lb.sb("rsb", [32, TPC], F32)
    rrow = lb.sb("rrow", [1, TPC], F32)
    rbb = lb.sb("rbb", [128, TPC], F32)
    onesf = lb.sb("onesf", [128, 128], F32)
    osb = lb.sb("osb", [128, TPC], BF16)
    wf32 = lb.sb("wf32", [128, DC, 128], F32)
    wbf = lb.sb("wbf", [128, DC, 128], BF16)
    stage = lb.sb("stage", [128, TPC], BF16)
    bcol = lb.sb("bcol", [128, 2 * DC], F32)
    hsb = lb.sb("hsb", [128, TPC], BF16)
    Bp32 = lb.sb("Bp32", [128, 128], F32)
    Bpb = lb.sb("Bpb", [128, 128], BF16)
    idxsb = lb.sb("idxsb", [128, TPC], U8)
    mk = lb.sb("mk", [128, 512], BF16)
    accf = lb.sb("accf", [128, 512], F32)
    tmpy = lb.sb("tmpy", [128, 512], F32)
    uf = lb.sb("uf", [128, 512], F32)
    gb = lb.sb("gb", [128, 512], BF16)
    sgf = lb.sb("sgf", [128, 512], F32)
    xin = lb.sb("xin", [128, TPC], F32)
    x1s = lb.sb("x1s", [128, TPC], F32)

    lb.memset(onesf[:], 1.0)
    lb.din(bcol[:, 0:DC], wob)
    lb.din(bcol[:, DC:2 * DC], b0c)

    lb.din(rsb[:], rc)
    lb.recip(rsb[:], rsb[:])
    lb.dout(scr_r, rsb[:])
    ntb = TPC // 512
    for h in range(32):
        lb.din(rrow[:], scr_r[h:h + 1, :])
        for tb in range(ntb):
            sl = slice(tb * 512, (tb + 1) * 512)
            lb.mm(psb[:, tb * 512:(tb + 1) * 512], onesf[0:1, :], rrow[:, sl],
                  start=True, stop=True, kind="mm_b")
            lb.cp(rbb[:, sl], psb[:, tb * 512:(tb + 1) * 512])
        lb.din(osb[:], oT[h * 128:(h + 1) * 128, :])
        lb.tt(bufA[:, h, :], osb[:], rbb[:], ALU.mult)

    def wo_evict(nb):
        for tb in range(TPC // 512):
            sl = slice(tb * 512, (tb + 1) * 512)
            lb.act(stage[:, sl], psb[:, sl], AF.Identity,
                   bias=bcol[:, nb:nb + 1], scale=1.0)
        lb.cp(bufB[:, nb, :], stage[:])

    proj_loop(lb, woT, bufA, psb, wf32, wbf, DC, wo_evict)

    for c in range(DC):
        lb.din(bufA[:, c, :], xnT[c * 128:(c + 1) * 128, :])

    chunked_w_dma(lb, wf32, Apk)
    lb.cp(wbf[:, :, :], wf32[:, :, :])
    for tb in range(ntb):
        for c in range(DC):
            lb.mm(psb[:, tb * 512:(tb + 1) * 512], wbf[:, c, :],
                  bufA[:, c, tb * 512:(tb + 1) * 512],
                  start=(c == 0), stop=(c == DC - 1))
    for tb in range(ntb):
        sl = slice(tb * 512, (tb + 1) * 512)
        lb.cp(hsb[:, sl], psb[:, sl], eng="scalar")

    dma_tok = chunked_w_dma(lb, wf32, W0T[:, 0:128])
    for ob in range(DC):
        cast_tok = lb.cp(wbf[:, :, :], wf32[:, :, :], extra=dma_tok)
        if ob + 1 < DC:
            dma_tok = chunked_w_dma(lb, wf32,
                                    W0T[:, (ob + 1) * 128:(ob + 2) * 128],
                                    wait=cast_tok, link=False)
        lb.din(idxsb[:], idxT[ob * 128:(ob + 1) * 128, :])
        lb.din(Bp32[:], Bpk[:, ob * 128:(ob + 1) * 128])
        lb.cp(Bpb[:], Bp32[:])
        lb.din(xin[:], xT[ob * 128:(ob + 1) * 128, :])
        for tb in range(ntb):
            sl = slice(tb * 512, (tb + 1) * 512)
            for c in range(DC):
                lb.mm(psb[:, 0:512], wbf[:, c, :], bufA[:, c, sl],
                      start=(c == 0), stop=(c == DC - 1))
            for l in range(4):
                tp = (96, 0) if l == 3 else None
                lb.mm(psb[:, 512 * (1 + l):512 * (1 + l) + 512],
                      Bpb[32 * l:32 * l + 16, :], hsb[32 * l:32 * l + 16, sl],
                      start=True, stop=True, tile_position=tp, kind="mm_y")
            for l in range(4):
                lb.ts(mk[:], idxsb[:, sl], float(l), None, ALU.is_equal)
                if l == 0:
                    lb.tt(accf[:], psb[:, 512:1024], mk[:], ALU.mult)
                else:
                    lb.tt(tmpy[:], psb[:, 512 * (1 + l):512 * (1 + l) + 512],
                          mk[:], ALU.mult)
                    lb.tt(accf[:], accf[:], tmpy[:], ALU.add)
            lb.stt(uf[:], accf[:], 8.0, psb[:, 0:512], ALU.mult, ALU.add)
            lb.act(gb[:], uf[:], AF.Silu, bias=bcol[:, DC + ob:DC + ob + 1],
                   scale=1.0)
            lb.tt(sgf[:], bufB[:, ob, sl], gb[:], ALU.mult)
            lb.tt(x1s[:, sl], xin[:, sl], sgf[:], ALU.add)
        lb.dout(x1T[ob * 128:(ob + 1) * 128, :], x1s[:])
    return lb


def build_l1pre():
    lb = LB()
    x1T = lb.dram_in("x1T", [D, TPC], F32)
    Wq = lb.dram_in("Wq", [D, D], F32)
    Wk = lb.dram_in("Wk", [D, D], F32)
    Wv = lb.dram_in("Wv", [D, D], F32)
    cosc = lb.dram_in("cosc", [64, TPC], F32)
    sinc = lb.dram_in("sinc", [64, TPC], F32)
    xn1T = lb.dram_out("xn1T", [D, TPC], BF16)
    q1T = lb.dram_out("q1T", [D, TPC], BF16)
    k1T = lb.dram_out("k1T", [D, TPC], BF16)
    v1n = lb.dram_out("v1n", [TPC, D], BF16)

    psb = lb.ps([128, 3584], F32)
    pst = lb.ps([128, 1024], BF16)
    xin = lb.sb("xin", [128, TPC], F32)
    sqf = lb.sb("sqf", [128, TPC], F32)
    rows = lb.sb("rows", [1, 6 * TPC + 8], F32)
    bcast = lb.sb("bcast", [128, 2 * TPC], F32)
    onesf = lb.sb("onesf", [128, 128], F32)
    xnsb = lb.sb("xnsb", [128, DC, TPC], BF16)
    wf32 = lb.sb("wf32", [128, DC, 128], F32)
    wbf = lb.sb("wbf", [128, DC, 128], BF16)
    stage = lb.sb("stage", [128, TPC], BF16)
    cs = lb.sb("cs", [64, 2 * TPC], F32)
    qrb = lb.sb("qrb", [64, TPC], BF16)
    qsw = lb.sb("qsw", [64, TPC], BF16)
    ta = lb.sb("ta", [64, TPC], BF16)
    tb2 = lb.sb("tb2", [64, TPC], BF16)
    ident = lb.sb("ident", [128, 128], BF16)
    iot = lb.sb("iot", [128, 128], I32)
    vns = lb.sb("vns", [128, 128], BF16)

    lb.memset(onesf[:], 1.0)
    emit_identity(lb, ident, iot)
    lb.din(cs[:, 0:TPC], cosc)
    lb.din(cs[:, TPC:2 * TPC], sinc)

    emit_ln(lb, x1T, xnsb, xn1T, psb, xin, sqf, rows, bcast, onesf)

    ntb = TPC // 512

    def proj_out(w_hbm, dst_hbm):
        def ev(nb):
            for tb in range(ntb):
                sl = slice(tb * 512, (tb + 1) * 512)
                lb.cp(stage[:, sl], psb[:, sl], eng="scalar")
            lb.dout(dst_hbm[nb * 128:(nb + 1) * 128, :], stage[:])
        proj_loop(lb, w_hbm, xnsb, psb, wf32, wbf, DC, ev)

    def rotary(dst_hbm):
        cosr = cs[:, 0:TPC]
        sinr = cs[:, TPC:2 * TPC]
        for h in range(16):
            r0 = 256 * h
            lb.din(qrb[:], dst_hbm[r0:r0 + 64, :])
            lb.din(qsw[0:32, :], qrb[32:64, :])
            lb.din(qsw[32:64, :], qrb[0:32, :])
            lb.tt(ta[:], qrb[:], cosr, ALU.mult)
            lb.tt(tb2[:], qsw[:], sinr, ALU.mult)
            lb.tt(qrb[:], ta[:], tb2[:], ALU.add)
            lb.dout(dst_hbm[r0:r0 + 64, :], qrb[:])

    proj_out(Wq, q1T)
    rotary(q1T)
    proj_out(Wk, k1T)
    rotary(k1T)
    def v_evict(nb):
        for tb in range(ntb):
            sl = slice(tb * 512, (tb + 1) * 512)
            lb.cp(stage[:, sl], psb[:, sl], eng="scalar")
        for t8 in range(TPC // 128):
            lb.trans(pst[:, 0:128], stage[:, t8 * 128:(t8 + 1) * 128], ident[:])
            lb.cp(vns[:], pst[:, 0:128])
            lb.dout(v1n[t8 * 128:(t8 + 1) * 128, nb * 128:(nb + 1) * 128],
                    vns[:])
    proj_loop(lb, Wv, xnsb, psb, wf32, wbf, DC, v_evict)
    return lb


NI = 11008


def build_l1post():
    lb = LB()
    oT = lb.dram_in("oT", [D, TPC], BF16)
    rc = lb.dram_in("rc", [16, TPC], F32)
    x1T = lb.dram_in("x1T", [D, TPC], F32)
    xn1T = lb.dram_in("xn1T", [D, TPC], BF16)
    Wo = lb.dram_in("Wo", [D, D], F32)
    gqw = lb.dram_in("gqw", [D, NI // 8], I32)
    gsc = lb.dram_in("gsc", [32, NI], F32)
    gqz = lb.dram_in("gqz", [32, NI // 8], I32)
    uqw = lb.dram_in("uqw", [D, NI // 8], I32)
    usc = lb.dram_in("usc", [32, NI], F32)
    uqz = lb.dram_in("uqz", [32, NI // 8], I32)
    dqw = lb.dram_in("dqw", [NI, D // 8], I32)
    dsc = lb.dram_in("dsc", [86, D], F32)
    dqz = lb.dram_in("dqz", [86, D // 8], I32)
    outT = lb.dram_out("outT", [D, TPC], F32)

    scr_r = lb.dram_scratch("scr_r", [16, TPC], F32)
    zr_g = lb.dram_scratch("zr_g", [32, NI], BF16)
    sr_g = lb.dram_scratch("sr_g", [32, NI], BF16)
    zr_u = lb.dram_scratch("zr_u", [32, NI], BF16)
    sr_u = lb.dram_scratch("sr_u", [32, NI], BF16)
    zr_d = lb.dram_scratch("zr_d", [86, D], BF16)
    sr_d = lb.dram_scratch("sr_d", [86, D], BF16)
    dq_g = lb.dram_scratch("dq_g", [D, NI], BF16)
    dq_u = lb.dram_scratch("dq_u", [D, NI], BF16)
    dq_d = lb.dram_scratch("dq_d", [NI, D], BF16)
    hTs = lb.dram_scratch("hTs", [NI, TPC], BF16)

    psb = lb.ps([128, 4096], F32)
    ntb = TPC // 512

    dq_scope = []
    qwsb = lb.sb("qwsb", [128, 2, 256], I32, scope=dq_scope)
    nibsb = lb.sb("nibsb", [128, 2048], I32, scope=dq_scope)
    t1b = lb.sb("t1b", [128, 2048], BF16, scope=dq_scope)
    ssbc = lb.sb("ssbc", [128, 2048], F32, scope=dq_scope)
    wrowc = lb.sb("wrowc", [128, 2, 2048], BF16, scope=dq_scope)
    zrow = lb.sb("zrow", [1, NI], BF16, scope=dq_scope)
    srow = lb.sb("srow", [1, NI], BF16, scope=dq_scope)
    onesb = lb.sb("onesb", [1, 128], BF16, scope=dq_scope)
    lb.memset(onesb[:], 1.0)

    def unpack8(dst_i32, src_i32, chw, extra=None):
        tok = None
        for j in range(8):
            dst = dst_i32[:, CPL[j]:chw:8]
            if j == 0:
                tok = lb.ts(dst, src_i32, 15, None, ALU.bitwise_and,
                            extra=extra)
            else:
                tok = lb.ts(dst, src_i32, 4 * j, 15,
                            ALU.logical_shift_right, ALU.bitwise_and)
        return tok

    def dequant(qw, qz, sc, zr, sr, dq, G, N):
        pch = 2048
        nzch = (N + pch - 1) // pch
        for ch in range(nzch):
            c0 = ch * pch
            w = min(pch, N - c0)
            lb.din(qwsb[0:G, 0, 0:w // 8], qz[:, c0 // 8:(c0 + w) // 8])
            unpack8(nibsb[0:G, 0:w], qwsb[0:G, 0, 0:w // 8], w)
            lb.cp(t1b[0:G, 0:w], nibsb[0:G, 0:w])
            lb.dout(zr[:, c0:c0 + w], t1b[0:G, 0:w])
            lb.din(ssbc[0:G, 0:w], sc[:, c0:c0 + w])
            lb.cp(t1b[0:G, 0:w], ssbc[0:G, 0:w])
            lb.dout(sr[:, c0:c0 + w], t1b[0:G, 0:w])
        chunks = []
        c0 = 0
        while c0 < N:
            w = min(2048, N - c0)
            chunks.append((c0, w))
            c0 += w
        nch = len(chunks)
        total = G * nch
        qw_tok = {}
        dout_toks = {}
        qw_tok[0] = lb.din(qwsb[:, 0, 0:chunks[0][1] // 8],
                           qw[0:128, 0:chunks[0][1] // 8])
        for g in range(G):
            lb.din(zrow[:, 0:N], zr[g:g + 1, :])
            lb.din(srow[:, 0:N], sr[g:g + 1, :])
            for ic, (c0, w) in enumerate(chunks):
                ci = g * nch + ic
                buf = ci % 2
                for seg in range((w + 511) // 512):
                    sw = min(512, w - seg * 512)
                    lb.mm(psb[:, seg * 512:seg * 512 + sw], onesb[:],
                          zrow[0:1, c0 + seg * 512:c0 + seg * 512 + sw],
                          start=True, stop=True, kind="mm_b")
                    lb.mm(psb[:, 2048 + seg * 512:2048 + seg * 512 + sw],
                          onesb[:],
                          srow[0:1, c0 + seg * 512:c0 + seg * 512 + sw],
                          start=True, stop=True, kind="mm_b")
                ts_tok = unpack8(nibsb[:, 0:w], qwsb[:, buf, 0:w // 8], w,
                                 extra=qw_tok.pop(buf, None))
                if ci + 1 < total:
                    gg, idx = divmod(ci + 1, nch)
                    nc0, nw = chunks[idx]
                    qw_tok[(ci + 1) % 2] = lb.din(
                        qwsb[:, (ci + 1) % 2, 0:nw // 8],
                        qw[gg * 128:(gg + 1) * 128, nc0 // 8:(nc0 + nw) // 8],
                        wait=ts_tok, link=False)
                lb.tt(t1b[:, 0:w], nibsb[:, 0:w], psb[:, 0:w], ALU.subtract)
                wr_tok = lb.tt(wrowc[:, buf, 0:w], t1b[:, 0:w],
                               psb[:, 2048:2048 + w], ALU.mult,
                               extra=dout_toks.get(buf))
                last = (ci == total - 1)
                dout_toks[buf] = lb.dout(dq[g * 128:(g + 1) * 128, c0:c0 + w],
                                         wrowc[:, buf, 0:w],
                                         wait=("chain" if last else wr_tok),
                                         link=last)

    dequant(gqw, gqz, gsc, zr_g, sr_g, dq_g, 32, NI)
    dequant(uqw, uqz, usc, zr_u, sr_u, dq_u, 32, NI)
    dequant(dqw, dqz, dsc, zr_d, sr_d, dq_d, 86, D)
    lb.close_scope(dq_scope)

    bufA = lb.sb("bufA", [128, DC, TPC], BF16)
    bufB = lb.sb("bufB", [128, DC, TPC], BF16)
    rsb = lb.sb("rsb", [16, TPC], F32)
    rrow = lb.sb("rrow", [1, TPC], F32)
    rbb = lb.sb("rbb", [128, TPC], F32)
    onesf = lb.sb("onesf", [128, 128], F32)
    osb = lb.sb("osb", [128, TPC], BF16)
    scB = []
    wf32 = lb.sb("wf32", [128, DC, 128], F32, scope=scB)
    wbf = lb.sb("wbf", [128, DC, 128], BF16, scope=scB)

    lb.memset(onesf[:], 1.0)

    lb.din(rsb[:], rc)
    lb.recip(rsb[:], rsb[:])
    lb.dout(scr_r, rsb[:])
    for h in range(16):
        lb.din(rrow[:], scr_r[h:h + 1, :])
        for tb in range(ntb):
            sl = slice(tb * 512, (tb + 1) * 512)
            lb.mm(psb[:, tb * 512:(tb + 1) * 512], onesf[0:1, :], rrow[:, sl],
                  start=True, stop=True, kind="mm_b")
            lb.cp(rbb[:, sl], psb[:, tb * 512:(tb + 1) * 512])
        for c2 in range(2):
            c = 2 * h + c2
            lb.din(osb[:], oT[c * 128:(c + 1) * 128, :])
            lb.tt(bufA[:, c, :], osb[:], rbb[:], ALU.mult)

    def wo1_evict(nb):
        for tb in range(ntb):
            sl = slice(tb * 512, (tb + 1) * 512)
            lb.cp(bufB[:, nb, sl], psb[:, sl], eng="scalar")
    proj_loop(lb, Wo, bufA, psb, wf32, wbf, DC, wo1_evict)
    lb.close_scope(scB)
    scD = []
    gsl = lb.sb("gsl", [128, 2, DC, 128], BF16, scope=scD)
    usl = lb.sb("usl", [128, 2, DC, 128], BF16, scope=scD)
    gsil = lb.sb("gsil", [128, TPC], BF16, scope=scD)
    hrow = lb.sb("hrow", [128, 2, TPC], BF16, scope=scD)

    for c in range(DC):
        lb.din(bufA[:, c, :], xn1T[c * 128:(c + 1) * 128, :])
    NIB = NI // 128

    def gu_din(Ib, buf, wait):
        gview = dq_g[:, Ib * 128:(Ib + 1) * 128].rearrange("(c p) n -> p c n", p=128)
        uview = dq_u[:, Ib * 128:(Ib + 1) * 128].rearrange("(c p) n -> p c n", p=128)
        tok = None
        st = DC // 4
        for gr in range(4):
            lb.din(gsl[:, buf, gr * st:(gr + 1) * st, :],
                   gview[:, gr * st:(gr + 1) * st, :],
                   wait=(wait if gr == 0 else None), link=(wait == "chain"))
            tok = lb.din(usl[:, buf, gr * st:(gr + 1) * st, :],
                         uview[:, gr * st:(gr + 1) * st, :],
                         wait=None, link=(wait == "chain"))
        return tok

    din_tok = gu_din(0, 0, "chain")
    hout_tok = {}
    for Ib in range(NIB):
        buf = Ib % 2
        first_mm_tok = None
        for tb in range(ntb):
            for c in range(DC):
                t = lb.mm(psb[:, tb * 512:(tb + 1) * 512], gsl[:, buf, c, :],
                          bufA[:, c, tb * 512:(tb + 1) * 512],
                          start=(c == 0), stop=(c == DC - 1),
                          extra=(din_tok if (tb == 0 and c == 0) else None))
                if first_mm_tok is None:
                    first_mm_tok = t
                lb.mm(psb[:, 1024 + tb * 512:1024 + (tb + 1) * 512],
                      usl[:, buf, c, :], bufA[:, c, tb * 512:(tb + 1) * 512],
                      start=(c == 0), stop=(c == DC - 1))
        if Ib + 1 < NIB:
            din_tok = gu_din(Ib + 1, (Ib + 1) % 2, first_mm_tok)
        hb = Ib % 2
        wr_tok = None
        for tb in range(ntb):
            sl = slice(tb * 512, (tb + 1) * 512)
            lb.act(gsil[:, sl], psb[:, sl], AF.Silu)
            wr_tok = lb.tt(hrow[:, hb, sl],
                           psb[:, 1024 + tb * 512:1024 + (tb + 1) * 512],
                           gsil[:, sl], ALU.mult,
                           extra=(hout_tok.get(hb) if tb == 0 else None))
        last = (Ib == NIB - 1)
        hout_tok[hb] = lb.dout(hTs[Ib * 128:(Ib + 1) * 128, :],
                               hrow[:, hb, :],
                               wait=("chain" if last else wr_tok), link=last)

    lb.close_scope(scD)
    scE = []
    hsl = lb.sb("hsl", [128, 2, TPC], BF16, scope=scE)
    dsl = lb.sb("dsl", [128, 2, 512], BF16, scope=scE)
    gf = lb.sb("gf", [128, 512], F32, scope=scE)
    sgf = lb.sb("sgf", [128, 512], F32, scope=scE)
    xin = lb.sb("xin", [128, TPC], F32, scope=scE)
    x1s = lb.sb("x1s", [128, TPC], F32, scope=scE)
    for ds in range(8):
        def hd_din(g, buf, wait):
            lb.din(hsl[:, buf, :], hTs[g * 128:(g + 1) * 128, :],
                   wait=wait, link=(wait == "chain"))
            return lb.din(dsl[:, buf, :],
                          dq_d[g * 128:(g + 1) * 128, ds * 512:(ds + 1) * 512],
                          wait=None, link=(wait == "chain"))
        din_tok = hd_din(0, 0, "chain")
        for g in range(NIB):
            buf = g % 2
            first_tok = None
            for db in range(4):
                for tb in range(ntb):
                    t = lb.mm(psb[:, (db * 2 + tb) * 512:(db * 2 + tb + 1) * 512],
                              dsl[:, buf, db * 128:(db + 1) * 128],
                              hsl[:, buf, tb * 512:(tb + 1) * 512],
                              start=(g == 0), stop=(g == NIB - 1),
                              extra=(din_tok if (db == 0 and tb == 0) else None))
                    if first_tok is None:
                        first_tok = t
            if g + 1 < NIB:
                din_tok = hd_din(g + 1, (g + 1) % 2, first_tok)
        for db in range(4):
            dblk = ds * 4 + db
            lb.din(xin[:], x1T[dblk * 128:(dblk + 1) * 128, :])
            for tb in range(ntb):
                sl = slice(tb * 512, (tb + 1) * 512)
                lb.act(gf[:], psb[:, (db * 2 + tb) * 512:(db * 2 + tb + 1) * 512],
                       AF.Silu)
                lb.tt(sgf[:], bufB[:, dblk, sl], gf[:], ALU.mult)
                lb.tt(x1s[:, sl], xin[:, sl], sgf[:], ALU.add)
            lb.dout(outT[dblk * 128:(dblk + 1) * 128, :], x1s[:])
    lb.close_scope(scE)
    return lb


def _run(lb, in_maps):
    nc = lb.finish()
    res = run_bass_kernel_spmd(nc, in_maps, core_ids=list(range(NCORES)))
    return res.results


def _cont(a):
    return np.ascontiguousarray(a)


def _threefry2x32(k1, k2, x0, x1):
    U = np.uint32

    def rotl(x, d):
        return (x << U(d)) | (x >> U(32 - d))

    ks0, ks1 = U(k1), U(k2)
    ks2 = ks0 ^ ks1 ^ U(0x1BD11BDA)
    x = [x0.astype(U) + ks0, x1.astype(U) + ks1]

    def rounds(x, rots):
        for r in rots:
            x[0] = x[0] + x[1]
            x[1] = rotl(x[1], r) ^ x[0]
        return x

    r0 = [13, 15, 26, 6]
    r1 = [17, 29, 16, 24]
    x = rounds(x, r0)
    x[0] += ks1
    x[1] += ks2 + U(1)
    x = rounds(x, r1)
    x[0] += ks2
    x[1] += ks0 + U(2)
    x = rounds(x, r0)
    x[0] += ks0
    x[1] += ks1 + U(3)
    x = rounds(x, r1)
    x[0] += ks1
    x[1] += ks2 + U(4)
    x = rounds(x, r0)
    x[0] += ks2
    x[1] += ks0 + U(5)
    return x[0], x[1]


def _mask_idx():
    n = B * T * D
    counts = np.arange(n, dtype=np.uint64)
    c_hi = (counts >> np.uint64(32)).astype(np.uint32)
    c_lo = (counts & np.uint64(0xFFFFFFFF)).astype(np.uint32)
    b1, b2 = _threefry2x32(np.uint32(0), np.uint32(42), c_hi, c_lo)
    bits = b1 ^ b2
    fb = (bits >> np.uint32(9)) | np.uint32(0x3F800000)
    u = fb.view(np.float32) - np.float32(1.0)
    idx = ((u >= 0.25).astype(np.uint8) + (u >= 0.5).astype(np.uint8)
           + (u >= 0.75).astype(np.uint8))
    return idx.reshape(B, T, D)


def _rot_tables():
    inv_freq = (1.0 / 10000.0 ** (np.arange(0, 64, 2, dtype=np.float32) / 64.0)
                ).astype(np.float32)
    t = np.arange(T, dtype=np.float32)
    sinu = np.einsum("i,j->ij", t, inv_freq).astype(np.float32)
    return np.sin(sinu).astype(np.float32), np.cos(sinu).astype(np.float32)


def _l1_perm():
    perm = np.zeros(D, dtype=np.int64)
    for h in range(16):
        base = 256 * h
        newo = ([base + 2 * i for i in range(32)]
                + [base + 2 * i + 1 for i in range(32)]
                + [base + i for i in range(64, 256)])
        perm[base:base + 256] = newo
    return perm


def kernel(**kw):
    return _forward(kw, dbg=None)


def _forward(kw, dbg=None):
    (L0_attn_x, L0_attn_wq_w, L0_attn_wq_b, L0_attn_wk_w, L0_attn_wk_b,
     L0_attn_wv_w, L0_attn_wv_b, L0_attn_wo_w, L0_attn_wo_b,
     L0_ffn_W0, L0_ffn_b0, L0_ffn_A, L0_ffn_B,
     L1_attn_Wq, L1_attn_Wk, L1_attn_Wv, L1_attn_Wo,
     L1_ffn_gate_qweight, L1_ffn_gate_scales, L1_ffn_gate_qzeros,
     L1_ffn_up_qweight, L1_ffn_up_scales, L1_ffn_up_qzeros,
     L1_ffn_down_qweight, L1_ffn_down_scales, L1_ffn_down_qzeros) = (
        kw[n] for n in (
            "L0_attn_x", "L0_attn_wq_w", "L0_attn_wq_b", "L0_attn_wk_w",
            "L0_attn_wk_b", "L0_attn_wv_w", "L0_attn_wv_b", "L0_attn_wo_w",
            "L0_attn_wo_b", "L0_ffn_W0", "L0_ffn_b0", "L0_ffn_A", "L0_ffn_B",
            "L1_attn_Wq", "L1_attn_Wk", "L1_attn_Wv", "L1_attn_Wo",
            "L1_ffn_gate_qweight", "L1_ffn_gate_scales", "L1_ffn_gate_qzeros",
            "L1_ffn_up_qweight", "L1_ffn_up_scales", "L1_ffn_up_qzeros",
            "L1_ffn_down_qweight", "L1_ffn_down_scales", "L1_ffn_down_qzeros"))
    f32 = np.float32

    x = np.asarray(L0_attn_x, dtype=f32)
    xT = _cont(x.reshape(TOK, D).T)

    wqT = _cont(np.asarray(L0_attn_wq_w, f32).T)
    wqb = _cont(np.asarray(L0_attn_wq_b, f32).reshape(DC, 128).T)
    wkT = _cont(np.asarray(L0_attn_wk_w, f32).T)
    wkb = _cont(np.asarray(L0_attn_wk_b, f32)[:, None])
    wvT = _cont(np.asarray(L0_attn_wv_w, f32).T)
    wvb = _cont(np.asarray(L0_attn_wv_b, f32)[:, None])

    lb = build_l0pre()
    in_maps = []
    for c in range(NCORES):
        in_maps.append({
            "xT": _cont(xT[:, c * TPC:(c + 1) * TPC]),
            "wqT": wqT, "wqb": wqb, "wkT": wkT, "wkb": wkb,
            "wvT": wvT, "wvb": wvb,
        })
    r1 = _run(lb, in_maps)
    xnT_g = np.concatenate([r1[c]["xnT"] for c in range(NCORES)], axis=1)
    qT_g = np.concatenate([r1[c]["qT"] for c in range(NCORES)], axis=1)
    kT_g = np.concatenate([r1[c]["kT"] for c in range(NCORES)], axis=1)
    vn_g = np.concatenate([r1[c]["vn"] for c in range(NCORES)], axis=0)
    if dbg is not None:
        dbg.update(xnT=xnT_g, qT=qT_g, kT=kT_g, vn=vn_g)
        if dbg.get("stop") == 1:
            return None

    dmask = np.triu(np.full((128, 128), -1e30, dtype=f32), k=1)
    lb = build_attn(8, 128, 1.0 / math.sqrt(128.0))
    in_maps = []
    for c in range(NCORES):
        q8 = np.zeros((8, 128, T), dtype=bfdt)
        k8 = np.zeros((8, 128, T), dtype=bfdt)
        v8 = np.zeros((8, T, 128), dtype=bfdt)
        for p in range(8):
            gpair = c * 8 + p
            b, h = gpair // 32, gpair % 32
            q8[p] = qT_g[h * 128:(h + 1) * 128, b * T:(b + 1) * T]
            k8[p] = kT_g[:, b * T:(b + 1) * T]
            v8[p] = vn_g[b * T:(b + 1) * T, :]
        in_maps.append({"q8": q8, "k8": k8, "v8": v8, "dmask": dmask})
    r2 = _run(lb, in_maps)

    oT_g = np.zeros((D, TOK), dtype=bfdt)
    r_g = np.zeros((32, TOK), dtype=f32)
    for c in range(NCORES):
        for p in range(8):
            gpair = c * 8 + p
            b, h = gpair // 32, gpair % 32
            oT_g[h * 128:(h + 1) * 128, b * T:(b + 1) * T] = r2[c]["o8"][p]
            r_g[h, b * T:(b + 1) * T] = _cont(r2[c]["r8"][p].T).reshape(T)
    if dbg is not None:
        dbg.update(oT=oT_g, r=r_g)
        if dbg.get("stop") == 2:
            return None

    woT = _cont(np.asarray(L0_attn_wo_w, f32).T)
    wob = _cont(np.asarray(L0_attn_wo_b, f32).reshape(DC, 128).T)
    W0T = _cont(np.asarray(L0_ffn_W0, f32).T)
    b0c = _cont(np.asarray(L0_ffn_b0, f32).reshape(DC, 128).T)
    A = np.asarray(L0_ffn_A, f32)
    Bm = np.asarray(L0_ffn_B, f32)
    Apk = np.zeros((D, 128), dtype=f32)
    Bpk = np.zeros((128, D), dtype=f32)
    for l in range(4):
        Apk[:, 32 * l:32 * l + 16] = A[l].T
        Bpk[32 * l:32 * l + 16, :] = Bm[l].T
    idx = _mask_idx()
    idxT_g = _cont(idx.reshape(TOK, D).T)

    lb = build_l0post()
    in_maps = []
    for c in range(NCORES):
        sl = slice(c * TPC, (c + 1) * TPC)
        in_maps.append({
            "oT": _cont(oT_g[:, sl]), "rc": _cont(r_g[:, sl]),
            "xT": _cont(xT[:, sl]), "xnT": _cont(xnT_g[:, sl]),
            "woT": woT, "wob": wob, "W0T": W0T, "b0c": b0c,
            "Apk": Apk, "Bpk": Bpk, "idxT": _cont(idxT_g[:, sl]),
        })
    r3 = _run(lb, in_maps)
    x1T_g = np.concatenate([r3[c]["x1T"] for c in range(NCORES)], axis=1)
    if dbg is not None:
        dbg.update(x1T=x1T_g)
        if dbg.get("stop") == 3:
            return None

    perm = _l1_perm()
    Wq = _cont(np.asarray(L1_attn_Wq, f32)[:, perm])
    Wk = _cont(np.asarray(L1_attn_Wk, f32)[:, perm])
    Wv = _cont(np.asarray(L1_attn_Wv, f32))
    sint, cost = _rot_tables()
    cosc_g = _cont(np.tile(np.concatenate([cost.T, cost.T], axis=0), (1, B)))
    sinc_g = _cont(np.tile(np.concatenate([sint.T, sint.T], axis=0), (1, B)))

    lb = build_l1pre()
    in_maps = []
    for c in range(NCORES):
        sl = slice(c * TPC, (c + 1) * TPC)
        in_maps.append({
            "x1T": _cont(x1T_g[:, sl]), "Wq": Wq, "Wk": Wk, "Wv": Wv,
            "cosc": _cont(cosc_g[:, sl]), "sinc": _cont(sinc_g[:, sl]),
        })
    r4 = _run(lb, in_maps)
    xn1T_g = np.concatenate([r4[c]["xn1T"] for c in range(NCORES)], axis=1)
    q1T_g = np.concatenate([r4[c]["q1T"] for c in range(NCORES)], axis=1)
    k1T_g = np.concatenate([r4[c]["k1T"] for c in range(NCORES)], axis=1)
    v1n_g = np.concatenate([r4[c]["v1n"] for c in range(NCORES)], axis=0)
    if dbg is not None:
        dbg.update(xn1T=xn1T_g, q1T=q1T_g, k1T=k1T_g, v1n=v1n_g)
        if dbg.get("stop") == 4:
            return None

    lb = build_attn(4, 256, 1.0 / 16.0)
    in_maps = []
    for c in range(NCORES):
        q4 = np.zeros((4, 256, T), dtype=bfdt)
        k4 = np.zeros((4, 256, T), dtype=bfdt)
        v4 = np.zeros((4, T, 256), dtype=bfdt)
        for p in range(4):
            gpair = c * 4 + p
            b, h = gpair // 16, gpair % 16
            q4[p] = q1T_g[h * 256:(h + 1) * 256, b * T:(b + 1) * T]
            k4[p] = k1T_g[h * 256:(h + 1) * 256, b * T:(b + 1) * T]
            v4[p] = v1n_g[b * T:(b + 1) * T, h * 256:(h + 1) * 256]
        in_maps.append({"q8": q4, "k8": k4, "v8": v4, "dmask": dmask})
    r5 = _run(lb, in_maps)

    o1T_g = np.zeros((D, TOK), dtype=bfdt)
    r1_g = np.zeros((16, TOK), dtype=f32)
    for c in range(NCORES):
        for p in range(4):
            gpair = c * 4 + p
            b, h = gpair // 16, gpair % 16
            o1T_g[h * 256:(h + 1) * 256, b * T:(b + 1) * T] = r5[c]["o8"][p]
            r1_g[h, b * T:(b + 1) * T] = _cont(r5[c]["r8"][p].T).reshape(T)
    if dbg is not None:
        dbg.update(o1T=o1T_g, r1=r1_g)
        if dbg.get("stop") == 5:
            return None

    Wo1 = _cont(np.asarray(L1_attn_Wo, f32))
    lb = build_l1post()
    in_maps = []
    for c in range(NCORES):
        sl = slice(c * TPC, (c + 1) * TPC)
        in_maps.append({
            "oT": _cont(o1T_g[:, sl]), "rc": _cont(r1_g[:, sl]),
            "x1T": _cont(x1T_g[:, sl]), "xn1T": _cont(xn1T_g[:, sl]),
            "Wo": Wo1,
            "gqw": np.asarray(L1_ffn_gate_qweight, np.int32),
            "gsc": np.asarray(L1_ffn_gate_scales, f32),
            "gqz": np.asarray(L1_ffn_gate_qzeros, np.int32),
            "uqw": np.asarray(L1_ffn_up_qweight, np.int32),
            "usc": np.asarray(L1_ffn_up_scales, f32),
            "uqz": np.asarray(L1_ffn_up_qzeros, np.int32),
            "dqw": np.asarray(L1_ffn_down_qweight, np.int32),
            "dsc": np.asarray(L1_ffn_down_scales, f32),
            "dqz": np.asarray(L1_ffn_down_qzeros, np.int32),
        })
    r6 = _run(lb, in_maps)
    outT = np.concatenate([r6[c]["outT"] for c in range(NCORES)], axis=1)
    return _cont(outT.T.astype(np.float32)).reshape(B, T, D)
